# version 03 (final)
# baseline (speedup 1.0000x reference)
from contextlib import ExitStack
import numpy as np
import ml_dtypes
import concourse.bass as bass
import concourse.mybir as mybir
from concourse.bass_utils import run_bass_kernel_spmd

F32 = mybir.dt.float32
BF16 = mybir.dt.bfloat16
I32 = mybir.dt.int32
AF = mybir.ActivationFunctionType
ALU = mybir.AluOpType
AXX = mybir.AxisListType.X

NCORES = 8
B, T, D = 2, 8192, 1024
TC = 2048
DKH, DVH = 256, 512
CH = 128
NCH = T // CH
NFC = D // 128
EPS = 1e-5
BF = np.dtype(ml_dtypes.bfloat16)


class _Chain:
    def __init__(self):
        self.calls = []

    def then_inc(self, sem, val=1):
        self.calls.append((sem, val))
        return self


class _Rec:
    def __init__(self, prog, name):
        object.__setattr__(self, "_p", prog)
        object.__setattr__(self, "_n", name)

    def __getattr__(self, meth):
        p, n = self._p, self._n

        def call(*a, **k):
            ch = _Chain()
            p[n].append((meth, a, k, ch))
            if meth != "wait_ge":
                p["_last_" + n] = ch
            return ch
        return call


class Sem:
    def __init__(self, nc, name):
        self.h = nc.alloc_semaphore(name)
        self.n = 0
        self.open = False

    def inc(self, chain, k=1):
        chain.then_inc(self.h, k)
        self.n += k
        self.open = True
        return self.n


def _aux_for_core(c):
    s = c % 4
    tok0 = s * TC
    i = np.arange(128)
    ident = (i[:, None] == i[None, :]).astype(BF)
    sl_neg = (-(i[:, None] > i[None, :]).astype(np.float32)).astype(BF)
    su_neg = (-(i[:, None] < i[None, :]).astype(np.float32)).astype(BF)
    triu_i = ((i[:, None] <= i[None, :]).astype(np.float32)).astype(BF)
    ki = np.arange(384)[:, None]
    qj = np.arange(256)[None, :]
    dist = qj + 128 - ki
    interior = ((dist > 0) & (dist <= 128) & (ki >= 1)).astype(np.float32)
    boundary = ((dist > 0) & (dist <= 128) & (ki >= 128)).astype(np.float32)
    var0 = boundary if s == 0 else interior
    lmask = np.stack([var0, interior], 0)
    lmask = lmask.reshape(2, 3, 128, 256).transpose(0, 2, 1, 3).reshape(2, 128, 768)
    gt = tok0 + np.arange(TC)
    lat = (np.arange(64)[:, None] < (gt[None, :] // 128)).astype(np.float32)
    dead_loc = (gt == 0).astype(np.float32)[None, :]
    dead_lat = (gt < 128).astype(np.float32)[None, :]
    halo_sel = np.zeros((128, 4), np.float32)
    if s > 0:
        halo_sel[:, s - 1] = 1.0
    return dict(ident=ident, sl_neg=sl_neg, su_neg=su_neg, triu_i=triu_i,
                lmask=lmask.astype(BF), lat_mask=lat.astype(BF),
                dead_loc=dead_loc.astype(BF), dead_lat=dead_lat.astype(BF),
                halo_sel=halo_sel)


def build(taps=(), upto=4, dbg=frozenset()):
    nc = bass.Bass("TRN2", target_bir_lowering=False, debug=False)

    def ein(name, shape, dt=F32):
        return nc.dram_tensor(name, list(shape), dt, kind="ExternalInput")

    xb = ein("xb", (T, D))
    xmy = ein("xmy", (TC, D))
    wq = ein("wq", (D, DKH)); wk = ein("wk", (D, DKH))
    wv = ein("wv", (D, DVH)); wb = ein("wb", (128, 8))
    lnw = ein("a_lnw", (128, 8)); lnb = ein("a_lnb", (128, 8))
    onw = ein("a_onw", (128, 4))
    wo = ein("wo", (2 * D, D))
    qkvw = ein("qkvw", (D, 3 * D)); qkvb = ein("qkvb", (3 * D,))
    qkvb_c = ein("qkvb_c", (128, 16))
    rqw = ein("rqw", (D, 256)); rqb = ein("rqb", (128, 2))
    rkvw = ein("rkvw", (D, 512)); rkvb = ein("rkvb", (512,))
    rkvb_c = ein("rkvb_c", (128, 4))
    routw = ein("routw", (256, D)); routb = ein("routb", (128, 8))
    outw = ein("outw", (D, D)); outb = ein("outb", (D,))
    glw = ein("glw", (64, 1)); grw = ein("grw", (64, 1))
    fgam = ein("fgam", (128, 8)); fbet = ein("fbet", (128, 8))
    f1q = ein("f1q", (D, 4096), I32); f1s = ein("f1s", (8, 4096))
    f1b = ein("f1b", (128, 32))
    f2q = ein("f2q", (4096, D), I32); f2s = ein("f2s", (32, D))
    f2b = ein("f2b", (D,))
    ident_i = ein("ident", (128, 128), BF16)
    slneg_i = ein("sl_neg", (128, 128), BF16)
    suneg_i = ein("su_neg", (128, 128), BF16)
    triui_i = ein("triu_i", (128, 128), BF16)
    lmask_i = ein("lmask", (2, 128, 768), BF16)
    latm_i = ein("lat_mask", (64, TC), BF16)
    dloc_i = ein("dead_loc", (1, TC), BF16)
    dlat_i = ein("dead_lat", (1, TC), BF16)
    hsel_i = ein("halo_sel", (128, 4))

    xout = nc.dram_tensor("xout", [TC, D], F32, kind="ExternalOutput")
    tap_t = {}
    for tp in taps:
        if tp in ("x1", "x2", "x3"):
            tap_t[tp] = nc.dram_tensor("tap_" + tp, [NFC * 128, TC], F32, kind="ExternalOutput")
        elif tp == "o":
            tap_t[tp] = nc.dram_tensor("tap_o", [T, DVH], BF16, kind="ExternalOutput")
        elif tp == "a0":
            tap_t[tp] = nc.dram_tensor("tap_a0", [128, 2048], F32, kind="ExternalOutput")
        elif tp == "pp":
            tap_t[tp] = nc.dram_tensor("tap_pp", [128, 8 * NCH], F32, kind="ExternalOutput")
        elif tp == "ss":
            tap_t[tp] = nc.dram_tensor("tap_ss", [NCH * 128, 2 * DVH], BF16, kind="ExternalOutput")
        elif tp == "br":
            tap_t[tp] = nc.dram_tensor("tap_br", [16, 8192], F32, kind="ExternalOutput")
        elif tp == "vv":
            tap_t[tp] = nc.dram_tensor("tap_vv", [NCH * 128, 2 * DVH], BF16, kind="ExternalOutput")
        elif tp == "b0":
            tap_t[tp] = nc.dram_tensor("tap_b0", [128, 16384], F32, kind="ExternalOutput")
        elif tp == "ar":
            tap_t[tp] = nc.dram_tensor("tap_ar", [16, 8192], F32, kind="ExternalOutput")
        elif tp == "arm":
            tap_t[tp] = nc.dram_tensor("tap_arm", [128, 64 * 14], F32, kind="ExternalOutput")
        elif tp == "axf":
            tap_t[tp] = nc.dram_tensor("tap_axf", [NFC * 128, 128], BF16, kind="ExternalOutput")

    xbf_d = nc.dram_tensor("xbf_d", [T, D], BF16)
    mybf_d = nc.dram_tensor("mybf_d", [TC, D], BF16)
    o_d = nc.dram_tensor("o_d", [T, DVH], BF16)
    og_d = nc.dram_tensor("og_d", [4 * T, DVH], BF16)
    og2_d = nc.dram_tensor("og2_d", [T, DVH], BF16)
    cc2i_d = nc.dram_tensor("cc2i_d", [D, 144], BF16)
    cc2o_d = nc.dram_tensor("cc2o_d", [4 * D, 144], BF16)
    row_d = nc.dram_tensor("row_d", [16, 8192], F32)
    w1_scr = nc.dram_tensor("w1_scr", [D, 4096], BF16)
    w2_scr = nc.dram_tensor("w2_scr", [4096, D], BF16)
    hi_fd = nc.dram_tensor("hi_fd", [NFC * 128, TC], BF16)
    lo_fd = nc.dram_tensor("lo_fd", [NFC * 128, TC], BF16)

    prog = {k: [] for k in ("sync", "scalar", "vector", "tensor", "gpsimd")}
    for _k in list(prog):
        prog["_last_" + _k] = None
    sy = _Rec(prog, "sync")
    sc = _Rec(prog, "scalar")
    ve = _Rec(prog, "vector")
    te = _Rec(prog, "tensor")
    gp = _Rec(prog, "gpsimd")

    s_ld = Sem(nc, "s_ld")
    s_w0 = Sem(nc, "s_w0")
    s_w1 = Sem(nc, "s_w1")
    s_gd = Sem(nc, "s_gd")
    s_pe = Sem(nc, "s_pe")
    s_dv = Sem(nc, "s_dv")
    s_ac = Sem(nc, "s_ac")
    s_cc = Sem(nc, "s_cc")
    s_st = Sem(nc, "s_st")
    WSEM = [s_w0, s_w1]
    _ISSUER = {}

    def dwait(rec, dsem):
        iss = _ISSUER[id(dsem)]
        if dsem.open:
            iss.wait_ge(dsem.h, dsem.n)
            dsem.open = False
        if rec is not iss:
            rec.wait_ge(dsem.h, dsem.n)

    for _s, _i in ((s_ld, sy), (s_w0, sy), (s_w1, sy), (s_gd, gp), (s_st, sy), (s_cc, gp)):
        _ISSUER[id(_s)] = _i

    par_guard = {}

    def phase_barrier():
        for ds in (s_ld, s_w0, s_w1, s_st):
            dwait(sy, ds)
        dwait(gp, s_gd)
        for r in (sy, gp, te, ve, sc):
            r.wait_ge(s_pe.h, s_pe.n)
            r.wait_ge(s_dv.h, s_dv.n)
            r.wait_ge(s_ac.h, s_ac.n)
        for r in (te, ve, sc, gp):
            dwait(r, s_ld)
        for r in (te, ve, sc, sy):
            dwait(r, s_gd)

    ident = nc.alloc_sbuf_tensor("identt", [128, 128], BF16)
    slneg = nc.alloc_sbuf_tensor("slnegt", [128, 128], BF16)
    suneg = nc.alloc_sbuf_tensor("sunegt", [128, 128], BF16)
    triui = nc.alloc_sbuf_tensor("triuit", [128, 128], BF16)
    ones_c = nc.alloc_sbuf_tensor("ones_c", [128, 1], BF16)
    ones_cf = nc.alloc_sbuf_tensor("ones_cf", [128, 1], F32)
    ones_r = nc.alloc_sbuf_tensor("ones_r", [1, 512], BF16)
    rmath = nc.alloc_sbuf_tensor("rmath", [128, 64 * 14], F32)
    rows_bf = nc.alloc_sbuf_tensor("rowsbf", [1, 4096], BF16)
    stage_r = nc.alloc_sbuf_tensor("stage_r", [1, 512], F32)
    f1b_col = nc.alloc_sbuf_tensor("f1b_col", [128, 32], F32)
    fgam_col = nc.alloc_sbuf_tensor("fgam_col", [128, 8], F32)
    fbet_col = nc.alloc_sbuf_tensor("fbet_col", [128, 8], F32)
    fgam_cb = nc.alloc_sbuf_tensor("fgam_cb", [128, 8], BF16)

    def RM(i):
        return rmath[:, i * 64:(i + 1) * 64]

    def rb(slot, lo, hi):
        base = 0 if slot == 0 else 2048
        return rows_bf[0:1, base + lo: base + hi]

    for dst, src in ((ident, ident_i), (slneg, slneg_i), (suneg, suneg_i), (triui, triui_i)):
        c = sy.dma_start(out=dst[:], in_=src.ap()); s_ld.inc(c, 16)
    ve.memset(ones_c[:], 1.0)
    ve.memset(ones_cf[:], 1.0)
    c = ve.memset(ones_r[:], 1.0)
    s_dv.inc(c)

    MML = [None]

    def MM(out, lhsT, rhs, start, stop):
        MML[0] = te.matmul(out, lhsT, rhs, start=start, stop=stop, skip_group_check=True)
        return MML[0]

    def rsqrt_into(dst, src, t1, t2):
        ch = prog["_last_vector"]
        if any(sem is s_dv.h for sem, _ in ch.calls):
            vv = s_dv.n
        else:
            vv = s_dv.inc(ch)
        sc.wait_ge(s_dv.h, vv)
        c = sc.activation(t1, src, AF.Sqrt)
        av = s_ac.inc(c)
        ve.wait_ge(s_ac.h, av)
        c = ve.reciprocal(dst, t1)
        rv = s_dv.inc(c)
        ve.wait_ge(s_dv.h, rv)
        return c

    def stats_rows(xsrc, ntok, psA, psB, sq_scr, f32src, rd1, rd2):
        te.wait_ge(s_dv.h, s_dv.n)
        sc.wait_ge(s_dv.h, s_dv.n)
        ng = ntok // 512
        onesv = ones_cf if f32src else ones_c
        for g in range(ng):
            gl, gh = g * 512, (g + 1) * 512
            for fc in range(NFC):
                MM(psA, onesv[:, 0:1], xsrc(fc, gl, gh), fc == 0, fc == NFC - 1)
            pv = s_pe.inc(MML[0])
            sc.wait_ge(s_pe.h, pv)
            c = sc.copy(stage_r[:], psA); av = s_ac.inc(c)
            te.wait_ge(s_ac.h, av)
            sy.wait_ge(s_ac.h, av)
            c = sy.dma_start(out=row_d[rd1:rd1 + 1, gl:gh], in_=stage_r[:])
            dmv = s_ld.inc(c, 16)
            dwait(sc, s_ld)
            for fc in range(NFC):
                c = sc.activation(sq_scr[:], xsrc(fc, gl, gh), AF.Square)
                av = s_ac.inc(c)
                te.wait_ge(s_ac.h, av)
                MM(psB, ones_c[:, 0:1], sq_scr[:], fc == 0, fc == NFC - 1)
                pv = s_pe.inc(MML[0])
                sc.wait_ge(s_pe.h, pv)
            c = sc.copy(stage_r[:], psB); av = s_ac.inc(c)
            sy.wait_ge(s_ac.h, av)
            c = sy.dma_start(out=row_d[rd2:rd2 + 1, gl:gh], in_=stage_r[:])
            dmv = s_ld.inc(c, 16)
            dwait(sc, s_ld)
            te.wait_ge(s_ac.h, av)
        dwait(sy, s_ld)

    def vsync(c):
        rv = s_dv.inc(c)
        ve.wait_ge(s_dv.h, rv)
        return c

    def lnrows(ntok, fused, kmul=1.0):
        nt = ntok // 128
        c = sy.dma_start(out=RM(0)[:, 0:nt],
                         in_=row_d[8:9, 0:ntok].rearrange("o (p n) -> p o n", p=128))
        s_ld.inc(c, 16)
        c = sy.dma_start(out=RM(1)[:, 0:nt],
                         in_=row_d[9:10, 0:ntok].rearrange("o (p n) -> p o n", p=128))
        s_ld.inc(c, 16)
        dwait(ve, s_ld)
        vsync(ve.tensor_scalar_mul(RM(2)[:, 0:nt], RM(0)[:, 0:nt], 1.0 / D))
        vsync(ve.tensor_scalar_mul(RM(3)[:, 0:nt], RM(1)[:, 0:nt], 1.0 / D))
        vsync(ve.tensor_mul(RM(8)[:, 0:nt], RM(2)[:, 0:nt], RM(2)[:, 0:nt]))
        vsync(ve.tensor_sub(RM(3)[:, 0:nt], RM(3)[:, 0:nt], RM(8)[:, 0:nt]))
        vsync(ve.tensor_scalar_add(RM(8)[:, 0:nt], RM(3)[:, 0:nt], EPS))
        rsqrt_into(RM(4)[:, 0:nt], RM(8)[:, 0:nt], RM(9)[:, 0:nt], RM(12)[:, 0:nt])
        if fused:
            vsync(ve.tensor_scalar(RM(8)[:, 0:nt], RM(3)[:, 0:nt], 1.0 + EPS, EPS * EPS,
                                   ALU.mult, ALU.add))
            rsqrt_into(RM(6)[:, 0:nt], RM(8)[:, 0:nt], RM(9)[:, 0:nt], RM(12)[:, 0:nt])
        else:
            vsync(ve.tensor_copy(RM(6)[:, 0:nt], RM(4)[:, 0:nt]))
        vsync(ve.scalar_tensor_tensor(RM(7)[:, 0:nt], RM(2)[:, 0:nt], -1.0, RM(6)[:, 0:nt],
                                      ALU.mult, ALU.mult))
        vsync(ve.tensor_scalar_add(RM(10)[:, 0:nt], RM(4)[:, 0:nt], 1.0))
        c = vsync(ve.scalar_tensor_tensor(RM(11)[:, 0:nt], RM(2)[:, 0:nt], -kmul,
                                          RM(4)[:, 0:nt], ALU.mult, ALU.mult))
        dvv = s_dv.n
        sy.wait_ge(s_dv.h, dvv)
        for i, t in enumerate((RM(6), RM(7), RM(10), RM(11))):
            c = sy.dma_start(out=row_d[10 + i:11 + i, 0:ntok],
                             in_=t[:, 0:nt].rearrange("p (o n) -> p o n", o=1))
            s_ld.inc(c, 16)
        dwait(sy, s_ld)
        dwait(ve, s_ld)
        dwait(sc, s_ld)

    def load_rowbf(slot, rdidx, ntok):
        dwait(gp, s_ld)
        c = gp.dma_start(out=rb(slot, 0, ntok), in_=row_d[rdidx:rdidx + 1, 0:ntok])
        v = s_gd.inc(c, 16)
        dwait(te, s_gd)
        dwait(ve, s_gd)
        return v

    def bcast_tile(dst, rdidx, lo, hi):
        dwait(gp, s_ld)
        c = gp.dma_start(out=dst.rearrange("p (o n) -> p o n", o=1),
                         in_=row_d[rdidx:rdidx + 1, lo:hi].partition_broadcast(128))
        return s_gd.inc(c, 16)

    stA = ExitStack()
    x_fm = stA.enter_context(nc.sbuf_tensor("x_fm", [128, NFC * T], BF16))
    rbA = stA.enter_context(nc.sbuf_tensor("rbA", [1, T], BF16))

    def xfm(fc, lo, hi):
        return x_fm[:, fc * T + lo: fc * T + hi]

    for i in range(4):
        c = gp.dma_start(out=xbf_d[i * 2048:(i + 1) * 2048, :], in_=xb[i * 2048:(i + 1) * 2048, :])
        s_gd.inc(c, 16)
    cast_gd = s_gd.n
    dwait(sy, s_gd)
    for fc in range(NFC):
        c = sy.dma_start(out=xfm(fc, 0, T), in_=xbf_d[:, fc * 128:(fc + 1) * 128], transpose=True)
        s_ld.inc(c, 16)

    wq_sb = stA.enter_context(nc.sbuf_tensor("wq_sb", [128, NFC * DKH], BF16))
    wk_sb = stA.enter_context(nc.sbuf_tensor("wk_sb", [128, NFC * DKH], BF16))
    wv_sb = stA.enter_context(nc.sbuf_tensor("wv_sb", [128, NFC * DVH], BF16))
    wb_sb = stA.enter_context(nc.sbuf_tensor("wb_sb", [128, NFC], BF16))
    lnw_c = stA.enter_context(nc.sbuf_tensor("lnw_c", [128, NFC], F32))
    lnb_c = stA.enter_context(nc.sbuf_tensor("lnb_c", [128, NFC], BF16))
    brow = stA.enter_context(nc.sbuf_tensor("brow", [1, 2 * DKH + DVH + 1], BF16))
    crow = stA.enter_context(nc.sbuf_tensor("crow", [1, 2 * DKH + DVH + 1], BF16))
    for (dst, src, n) in ((wq_sb, wq, DKH), (wk_sb, wk, DKH), (wv_sb, wv, DVH)):
        c = gp.dma_start(out=dst[:].rearrange("p (c n) -> p c n", c=NFC),
                         in_=src.ap().rearrange("(c p) n -> p c n", p=128))
        s_gd.inc(c, 16)
    c = gp.dma_start(out=wb_sb[:], in_=wb.ap())
    s_gd.inc(c, 16)
    c = gp.dma_start(out=lnb_c[:], in_=lnb.ap())
    wts_gd = s_gd.inc(c, 16)
    c = sy.dma_start(out=lnw_c[:], in_=lnw.ap())
    s_ld.inc(c, 16)

    stPA = ExitStack()
    pool = stPA.enter_context(nc.psum_tensor("poolA", [128, 3584], F32))
    poolb = stPA.enter_context(nc.psum_tensor("poolAb", [128, 512], BF16))
    pS = pool[:, 0:1024]
    pQ = pool[:, 1024:1536]
    pO = pool[:, 1536:2048]
    pT1 = pool[:, 2048:2560]
    pV = pool[:, 2560:3072]
    pT2 = pool[:, 3072:3584]
    pR1 = pool[0:1, 1024:1536]
    pR2 = pool[0:1, 1536:2048]

    dwait(te, s_gd)
    CO = {"q": 0, "k": DKH, "v": 2 * DKH, "b": 2 * DKH + DVH}
    for (wsb, n, key) in ((wq_sb, DKH, "q"), (wk_sb, DKH, "k"), (wv_sb, DVH, "v"), (wb_sb, 1, "b")):
        for fc in range(NFC):
            MM(pR1[0:1, 0:n], lnb_c[:, fc:fc + 1], wsb[:, fc * n:(fc + 1) * n],
               fc == 0, fc == NFC - 1)
        pv = s_pe.inc(MML[0])
        sc.wait_ge(s_pe.h, pv)
        c = sc.copy(brow[0:1, CO[key]:CO[key] + n], pR1[0:1, 0:n]); av = s_ac.inc(c)
        te.wait_ge(s_ac.h, av)
    ve.wait_ge(s_ac.h, s_ac.n)
    dwait(ve, s_ld)
    for (wsb, n) in ((wq_sb, DKH), (wk_sb, DKH), (wv_sb, DVH), (wb_sb, 1)):
        for fc in range(NFC):
            ve.tensor_scalar_mul(wsb[:, fc * n:(fc + 1) * n], wsb[:, fc * n:(fc + 1) * n],
                                 lnw_c[:, fc:fc + 1])
    c = ve.engine_nop(); fold_dv = s_dv.inc(c)
    te.wait_ge(s_dv.h, fold_dv)
    for (wsb, n, key) in ((wq_sb, DKH, "q"), (wk_sb, DKH, "k"), (wv_sb, DVH, "v"), (wb_sb, 1, "b")):
        for fc in range(NFC):
            MM(pR1[0:1, 0:n], ones_c[:, 0:1], wsb[:, fc * n:(fc + 1) * n], fc == 0, fc == NFC - 1)
        pv = s_pe.inc(MML[0])
        sc.wait_ge(s_pe.h, pv)
        c = sc.copy(crow[0:1, CO[key]:CO[key] + n], pR1[0:1, 0:n]); av = s_ac.inc(c)
        te.wait_ge(s_ac.h, av)
    crow_ac = s_ac.n

    sqA = stA.enter_context(nc.sbuf_tensor("sqA", [128, 512], BF16))
    dwait(te, s_ld)
    stats_rows(xfm, T, pR1[0:1, 0:512], pR2[0:1, 0:512], sqA, False, 8, 9)
    lnrows(T, fused=True)
    if "ar" in taps:
        dwait(sy, s_ld)
        c = sy.dma_start(out=tap_t["ar"].ap(), in_=row_d.ap()[0:16, :])
        s_st.inc(c, 16)
    if "arm" in taps:
        dwait(sy, s_ld)
        c = sy.dma_start(out=tap_t["arm"].ap(), in_=rmath[:])
        s_st.inc(c, 16)
    dwait(gp, s_ld)
    c = gp.dma_start(out=rbA[:], in_=row_d[11:12, 0:T]); s_gd.inc(c, 16)
    dwait(ve, s_gd)
    dwait(te, s_gd)
    dwait(ve, s_ld)

    Rb_a = stA.enter_context(nc.sbuf_tensor("Rb_a", [128, 1024], BF16))
    for sec in range(8):
        lo, hi = sec * 1024, (sec + 1) * 1024
        bv = bcast_tile(Rb_a[:], 10, lo, hi)
        dwait(ve, s_gd)
        for fc in range(NFC):
            c = ve.tensor_mul(xfm(fc, lo, hi), xfm(fc, lo, hi), Rb_a[:])
        dvv = s_dv.inc(c)
        gp.wait_ge(s_dv.h, dvv)
    c = ve.engine_nop(); xs_dv = s_dv.inc(c)
    if "axf" in taps:
        sy.wait_ge(s_dv.h, xs_dv)
        for fc in range(NFC):
            c = sy.dma_start(out=tap_t["axf"].ap()[fc * 128:(fc + 1) * 128, :],
                             in_=xfm(fc, 0, 128))
            s_st.inc(c, 16)

    qs_t = stA.enter_context(nc.sbuf_tensor("qs_t", [128, DKH], BF16))
    ks_t = stA.enter_context(nc.sbuf_tensor("ks_t", [128, DKH], BF16))
    q_tok = stA.enter_context(nc.sbuf_tensor("q_tok", [128, DKH], BF16))
    k_tok = stA.enter_context(nc.sbuf_tensor("k_tok", [128, DKH], BF16))
    kb_tok = stA.enter_context(nc.sbuf_tensor("kb_tok", [128, DKH], BF16))
    vb_tok = stA.enter_context(nc.sbuf_tensor("vb_tok", [128, DVH], BF16))
    q_fm = stA.enter_context(nc.sbuf_tensor("q_fmA", [128, DKH], BF16))
    k_fm = stA.enter_context(nc.sbuf_tensor("k_fmA", [128, DKH], BF16))
    kb_fm = stA.enter_context(nc.sbuf_tensor("kb_fmA", [128, DKH], BF16))
    A_bf = stA.enter_context(nc.sbuf_tensor("A_bf", [128, 128], BF16))
    At_bf = stA.enter_context(nc.sbuf_tensor("At_bf", [128, 128], BF16))
    Bp = [stA.enter_context(nc.sbuf_tensor(f"Bp{i}", [128, 128], BF16)) for i in range(2)]
    Btp = [stA.enter_context(nc.sbuf_tensor(f"Btp{i}", [128, 128], BF16)) for i in range(2)]
    G2 = [stA.enter_context(nc.sbuf_tensor(f"G2_{i}", [128, 128], BF16)) for i in range(2)]
    W_fm = stA.enter_context(nc.sbuf_tensor("W_fmA", [128, DKH], BF16))
    U_sb = stA.enter_context(nc.sbuf_tensor("U_sbA", [128, DVH], F32))
    vnew = stA.enter_context(nc.sbuf_tensor("vnewA", [128, DVH], BF16))
    Pq_bf = stA.enter_context(nc.sbuf_tensor("Pq_bf", [128, 128], BF16))
    C1t_bf = stA.enter_context(nc.sbuf_tensor("C1t_bf", [128, 128], BF16))
    Gt_bf = stA.enter_context(nc.sbuf_tensor("Gt_bf", [128, 128], BF16))
    Et_bf = stA.enter_context(nc.sbuf_tensor("Et_bf", [128, 128], BF16))
    S_bf = stA.enter_context(nc.sbuf_tensor("S_bfA", [128, 2 * DVH], BF16))
    o_st = stA.enter_context(nc.sbuf_tensor("o_stA", [128, 2 * DVH], BF16))
    l2s = stA.enter_context(nc.sbuf_tensor("l2sA", [128, DVH], F32))
    pp = stA.enter_context(nc.sbuf_tensor("ppA", [128, 8], F32))
    beta_t = stA.enter_context(nc.sbuf_tensor("beta_tA", [128, 1], F32))

    ve.memset(S_bf[:], 0.0)
    c = ve.engine_nop(); dv_s_prev = s_dv.inc(c)
    te.wait_ge(s_dv.h, xs_dv)
    te.wait_ge(s_ac.h, crow_ac)

    ac_b_prev = s_ac.n
    dv_u_prev = s_dv.n
    dv_o_prev = s_dv.n
    dv_pq_prev = s_dv.n
    av6_prev = s_dv.n
    old_dma = s_ld.n

    for ci in range(NCH if "noscan" not in dbg else 0):
        lo, hi = ci * CH, (ci + 1) * CH
        te.wait_ge(s_ac.h, ac_b_prev)
        for fc in range(NFC):
            MM(pQ[:, 0:DKH], xfm(fc, lo, hi), wq_sb[:, fc * DKH:(fc + 1) * DKH], fc == 0, False)
        MM(pQ[:, 0:DKH], rbA[0:1, lo:hi], crow[0:1, CO["q"]:CO["q"] + DKH], False, False)
        MM(pQ[:, 0:DKH], ones_r[0:1, 0:CH], brow[0:1, CO["q"]:CO["q"] + DKH], False, True)
        pe_q = s_pe.inc(MML[0])
        sc.wait_ge(s_pe.h, pe_q)
        c = sc.activation(qs_t[:], pQ[:, 0:DKH], AF.Silu); ac_siq = s_ac.inc(c)
        c = sc.activation(l2s[:, 0:DKH], qs_t[:], AF.Square); ac_q = s_ac.inc(c)
        te.wait_ge(s_ac.h, ac_siq)
        for fc in range(NFC):
            MM(pQ[:, DKH:2 * DKH], xfm(fc, lo, hi), wk_sb[:, fc * DKH:(fc + 1) * DKH],
               fc == 0, False)
        MM(pQ[:, DKH:2 * DKH], rbA[0:1, lo:hi], crow[0:1, CO["k"]:CO["k"] + DKH], False, False)
        MM(pQ[:, DKH:2 * DKH], ones_r[0:1, 0:CH], brow[0:1, CO["k"]:CO["k"] + DKH], False, True)
        pe_k = s_pe.inc(MML[0])
        sc.wait_ge(s_pe.h, pe_k)
        c = sc.activation(ks_t[:], pQ[:, DKH:2 * DKH], AF.Silu); ac_sik = s_ac.inc(c)
        c = sc.activation(l2s[:, DKH:2 * DKH], ks_t[:], AF.Square); ac_k = s_ac.inc(c)
        te.wait_ge(s_ac.h, ac_sik)
        for fc in range(NFC):
            MM(pQ[:, DKH:DKH + 1], xfm(fc, lo, hi), wb_sb[:, fc:fc + 1], fc == 0, False)
        MM(pQ[:, DKH:DKH + 1], rbA[0:1, lo:hi], crow[0:1, CO["b"]:CO["b"] + 1], False, False)
        MM(pQ[:, DKH:DKH + 1], ones_r[0:1, 0:CH], brow[0:1, CO["b"]:CO["b"] + 1], False, True)
        pe_b = s_pe.inc(MML[0])
        sc.wait_ge(s_pe.h, pe_b)
        c = sc.activation(beta_t[:], pQ[:, DKH:DKH + 1], AF.Sigmoid); ac_b = s_ac.inc(c)
        ac_b_prev = ac_b
        te.wait_ge(s_dv.h, dv_u_prev)
        for fc in range(NFC):
            MM(pV[:, 0:DVH], xfm(fc, lo, hi), wv_sb[:, fc * DVH:(fc + 1) * DVH], fc == 0, False)
        MM(pV[:, 0:DVH], rbA[0:1, lo:hi], crow[0:1, CO["v"]:CO["v"] + DVH], False, False)
        MM(pV[:, 0:DVH], ones_r[0:1, 0:CH], brow[0:1, CO["v"]:CO["v"] + DVH], False, True)
        pe_v = s_pe.inc(MML[0])
        sc.wait_ge(s_pe.h, pe_v)
        c = sc.activation(vb_tok[:], pV[:, 0:DVH], AF.Silu); ac_v = s_ac.inc(c)

        ve.wait_ge(s_ac.h, ac_q)
        c = ve.tensor_reduce(pp[:, 0:1], l2s[:, 0:DKH], AXX, ALU.add)
        rv_ = s_dv.inc(c)
        ve.wait_ge(s_dv.h, rv_)
        ve.tensor_scalar_add(pp[:, 2:3], pp[:, 0:1], 1e-6)
        rsqrt_into(pp[:, 3:4], pp[:, 2:3], pp[:, 4:5], pp[:, 5:6])
        c = ve.tensor_scalar(q_tok[:], qs_t[:], pp[:, 3:4], DKH ** -0.5, ALU.mult, ALU.mult)
        dv_q = s_dv.inc(c)
        ve.wait_ge(s_ac.h, ac_k)
        c = ve.tensor_reduce(pp[:, 1:2], l2s[:, DKH:2 * DKH], AXX, ALU.add)
        rv_ = s_dv.inc(c)
        ve.wait_ge(s_dv.h, rv_)
        ve.tensor_scalar_add(pp[:, 2:3], pp[:, 1:2], 1e-6)
        rsqrt_into(pp[:, 3:4], pp[:, 2:3], pp[:, 4:5], pp[:, 5:6])
        c = ve.tensor_scalar_mul(k_tok[:], ks_t[:], pp[:, 3:4]); dv_k = s_dv.inc(c)
        ve.wait_ge(s_ac.h, ac_b)
        c = ve.tensor_scalar_mul(kb_tok[:], k_tok[:], beta_t[:, 0:1]); dv_kb = s_dv.inc(c)
        ve.wait_ge(s_ac.h, ac_v)
        c = ve.tensor_scalar_mul(vb_tok[:], vb_tok[:], beta_t[:, 0:1]); dv_vb = s_dv.inc(c)

        te.wait_ge(s_dv.h, dv_kb)
        for (tsrc, tdst) in ((q_tok, q_fm), (k_tok, k_fm), (kb_tok, kb_fm)):
            te.transpose(poolb[:, 0:128], tsrc[:, 0:128], ident[:])
            c = te.transpose(poolb[:, 128:256], tsrc[:, 128:256], ident[:])
            pv = s_pe.inc(c)
            ve.wait_ge(s_pe.h, pv)
            ve.tensor_copy(tdst[:, 0:128], poolb[:, 0:128])
            c = ve.tensor_copy(tdst[:, 128:256], poolb[:, 128:256])
            dv_tp = s_dv.inc(c)
            te.wait_ge(s_dv.h, dv_tp)

        te.wait_ge(s_dv.h, av6_prev)
        MM(pT1[:, 0:128], kb_fm[:, 0:128], k_fm[:, 0:128], True, False)
        MM(pT1[:, 0:128], kb_fm[:, 128:256], k_fm[:, 128:256], False, True)
        MM(pT1[:, 128:256], k_fm[:, 0:128], kb_fm[:, 0:128], True, False)
        MM(pT1[:, 128:256], k_fm[:, 128:256], kb_fm[:, 128:256], False, True)
        pe_gram = s_pe.inc(MML[0])
        ve.wait_ge(s_pe.h, pe_gram)
        ve.tensor_mul(A_bf[:], pT1[:, 0:128], slneg[:])
        ve.tensor_mul(At_bf[:], pT1[:, 128:256], suneg[:])
        c = ve.tensor_add(G2[0][:], At_bf[:], ident[:]); dv_g = s_dv.inc(c)
        curB, curBt, curG = A_bf, At_bf, G2[0]
        gpar = 0
        te.wait_ge(s_dv.h, dv_g)
        for sti in range(1, 7):
            nb, nbt = Bp[sti % 2], Btp[sti % 2]
            MM(pT1[:, 0:128], curBt[:], curB[:], True, True)
            if sti < 6:
                MM(pT1[:, 128:256], curB[:], curBt[:], True, True)
            pv = s_pe.inc(MML[0])
            ve.wait_ge(s_pe.h, pv)
            c = ve.tensor_copy(nb[:], pT1[:, 0:128])
            if sti < 6:
                c = ve.tensor_copy(nbt[:], pT1[:, 128:256])
            av = s_dv.inc(c)
            te.wait_ge(s_dv.h, av)
            MM(pT2[:, 0:128], nb[:], curG[:], True, True)
            pv = s_pe.inc(MML[0])
            ve.wait_ge(s_pe.h, pv)
            ngt = G2[1 - gpar]
            c = ve.tensor_add(ngt[:], pT2[:, 0:128], curG[:]); dv_g = s_dv.inc(c)
            te.wait_ge(s_dv.h, dv_g)
            curB, curBt, curG = nb, nbt, ngt
            gpar = 1 - gpar
            if sti == 6:
                av6_prev = av
        c = ve.tensor_sub(C1t_bf[:], ident[:], At_bf[:]); dv_c1 = s_dv.inc(c)
        te.wait_ge(s_dv.h, dv_c1)
        for nit in range(2):
            c = te.transpose(poolb[:, 256:384], curG[:], ident[:])
            pv = s_pe.inc(c)
            ve.wait_ge(s_pe.h, pv)
            c = ve.tensor_copy(Gt_bf[:], poolb[:, 256:384]); dvv = s_dv.inc(c)
            te.wait_ge(s_dv.h, dvv)
            MM(pT1[:, 0:128], C1t_bf[:], Gt_bf[:], True, True)
            pv = s_pe.inc(MML[0])
            ve.wait_ge(s_pe.h, pv)
            c = ve.tensor_sub(Et_bf[:], ident[:], pT1[:, 0:128]); dvv = s_dv.inc(c)
            te.wait_ge(s_dv.h, dvv)
            MM(pT2[:, 0:128], Et_bf[:], curG[:], True, True)
            pv = s_pe.inc(MML[0])
            ve.wait_ge(s_pe.h, pv)
            ngt = G2[1 - gpar]
            c = ve.tensor_add(ngt[:], pT2[:, 0:128], curG[:]); dvv = s_dv.inc(c)
            te.wait_ge(s_dv.h, dvv)
            curG = ngt
            gpar = 1 - gpar
        G = curG

        te.wait_ge(s_dv.h, dv_pq_prev)
        MM(pT2[:, 128:256], kb_tok[:, 0:128], G[:], True, True)
        MM(pT2[:, 256:384], kb_tok[:, 128:256], G[:], True, True)
        pv = s_pe.inc(MML[0])
        ve.wait_ge(s_pe.h, pv)
        ve.tensor_copy(W_fm[:, 0:128], pT2[:, 128:256])
        c = ve.tensor_copy(W_fm[:, 128:256], pT2[:, 256:384]); dv_w = s_dv.inc(c)
        te.wait_ge(s_ac.h, ac_v)
        te.wait_ge(s_dv.h, dv_vb)
        MM(pV[:, 0:DVH], G[:], vb_tok[:], True, True)
        pv = s_pe.inc(MML[0])
        ve.wait_ge(s_pe.h, pv)
        c = ve.tensor_copy(U_sb[:], pV[:, 0:DVH]); dv_u = s_dv.inc(c)
        dv_u_prev = dv_u

        te.wait_ge(s_dv.h, dv_w)
        te.wait_ge(s_dv.h, dv_s_prev)
        MM(pT2[:, 0:DVH], W_fm[:, 0:128], S_bf[:, 0:DVH], True, False)
        MM(pT2[:, 0:DVH], W_fm[:, 128:256], S_bf[:, DVH:2 * DVH], False, True)
        pe_ws = s_pe.inc(MML[0])
        ve.wait_ge(s_pe.h, pe_ws)
        ve.wait_ge(s_dv.h, dv_u)
        c = ve.tensor_sub(vnew[:], U_sb[:], pT2[:, 0:DVH]); dv_vn = s_dv.inc(c)

        te.wait_ge(s_dv.h, dv_o_prev)
        te.wait_ge(s_dv.h, dv_q)
        MM(pO[:, 0:DVH], q_fm[:, 0:128], S_bf[:, 0:DVH], True, False)
        MM(pO[:, 0:DVH], q_fm[:, 128:256], S_bf[:, DVH:2 * DVH], False, False)
        te.wait_ge(s_dv.h, dv_vn)
        MM(pT2[:, 128:256], k_fm[:, 0:128], q_fm[:, 0:128], True, False)
        MM(pT2[:, 128:256], k_fm[:, 128:256], q_fm[:, 128:256], False, True)
        pe_pq = s_pe.inc(MML[0])
        ve.wait_ge(s_pe.h, pe_pq)
        c = ve.tensor_mul(Pq_bf[:], pT2[:, 128:256], triui[:]); dv_pq = s_dv.inc(c)
        dv_pq_prev = dv_pq
        te.wait_ge(s_dv.h, dv_pq)
        MM(pO[:, 0:DVH], Pq_bf[:], vnew[:], False, True)
        pe_o = s_pe.inc(MML[0])

        MM(pS[:, 0:DVH], k_tok[:, 0:128], vnew[:], ci == 0, ci == NCH - 1)
        MM(pS[:, DVH:2 * DVH], k_tok[:, 128:256], vnew[:], ci == 0, ci == NCH - 1)
        pe_ds = s_pe.inc(MML[0])
        ve.wait_ge(s_pe.h, pe_ds)
        ve.tensor_copy(S_bf[:, 0:DVH], pS[:, 0:DVH])
        c = ve.tensor_copy(S_bf[:, DVH:2 * DVH], pS[:, DVH:2 * DVH])
        dv_s_prev = s_dv.inc(c)

        if ci == 0 and "a0" in taps:
            ve.wait_ge(s_pe.h, pe_o)
            c = ve.engine_nop(); _a0 = s_dv.inc(c)
            gp.wait_ge(s_dv.h, _a0)
            for (tsl, ap0) in ((slice(0, 256), q_tok[:]), (slice(256, 512), k_tok[:]),
                               (slice(512, 1024), vnew[:]), (slice(1024, 1152), G[:]),
                               (slice(1152, 1280), Pq_bf[:]), (slice(1280, 1408), A_bf[:])):
                c = gp.dma_start(out=tap_t["a0"].ap()[:, tsl], in_=ap0)
                s_gd.inc(c, 16)
            dwait(gp, s_gd)
        sc.wait_ge(s_pe.h, pe_o)
        c = sc.activation(l2s[:, 0:DVH], pO[:, 0:DVH], AF.Square)
        av = s_ac.inc(c)
        ve.wait_ge(s_ac.h, av)
        dwait(ve, s_ld)
        c = ve.tensor_reduce(pp[:, 6:7], l2s[:, 0:DVH], AXX, ALU.add)
        rv_ = s_dv.inc(c)
        ve.wait_ge(s_dv.h, rv_)
        ve.tensor_scalar(pp[:, 2:3], pp[:, 6:7], 1.0 / DVH, EPS, ALU.mult, ALU.add)
        rsqrt_into(pp[:, 3:4], pp[:, 2:3], pp[:, 4:5], pp[:, 5:6])
        c = ve.tensor_scalar_mul(o_st[:, (ci % 2) * DVH:(ci % 2 + 1) * DVH], pO[:, 0:DVH],
                                 pp[:, 3:4])
        dv_o = s_dv.inc(c)
        dv_o_prev = dv_o
        if "vv" in taps:
            gp.wait_ge(s_dv.h, dv_vn)
            c = gp.dma_start(out=tap_t["vv"].ap()[ci * 128:(ci + 1) * 128, 0:DVH], in_=vnew[:])
            s_gd.inc(c, 16)
            c = gp.dma_start(out=tap_t["vv"].ap()[ci * 128:(ci + 1) * 128, DVH:DVH + 256],
                             in_=W_fm[:])
            s_gd.inc(c, 16)
            c = gp.dma_start(out=tap_t["vv"].ap()[ci * 128:(ci + 1) * 128, DVH + 256:DVH + 384],
                             in_=k_tok[:, 0:128])
            s_gd.inc(c, 16)
            dwait(gp, s_gd)
        if "ss" in taps:
            gp.wait_ge(s_dv.h, dv_s_prev)
            c = gp.dma_start(out=tap_t["ss"].ap()[ci * 128:(ci + 1) * 128, :], in_=S_bf[:])
            s_gd.inc(c, 16)
            dwait(gp, s_gd)
        if "pp" in taps:
            gp.wait_ge(s_dv.h, dv_o)
            c = gp.dma_start(out=tap_t["pp"].ap()[:, ci * 8:(ci + 1) * 8], in_=pp[:])
            s_gd.inc(c, 16)
            dwait(gp, s_gd)
        if ci % 2 == 1:
            sy.wait_ge(s_dv.h, dv_o)
            grp = ci // 2
            c = sy.dma_start(out=o_d[grp * 256:(grp + 1) * 256, :].rearrange("(g p) n -> p g n", p=128),
                             in_=o_st[:].rearrange("p (g n) -> p g n", g=2))
            old_dma = s_ld.inc(c, 16)

    if "o" in taps:
        dwait(sy, s_ld)
        c = sy.dma_start(out=tap_t["o"].ap(), in_=o_d.ap()); s_st.inc(c, 16)

    cc1 = 0
    if "nocc" not in dbg:
        dwait(gp, s_ld)
        c = gp.collective_compute("AllGather", ALU.bypass,
                                  replica_groups=[[0, 1, 2, 3], [4, 5, 6, 7]],
                                  ins=[o_d.ap().opt()], outs=[og_d.ap().opt()])
        cc1 = s_cc.inc(c)
    stPA.close()
    stA.close()
    phase_barrier()

    stream = nc.alloc_sbuf_tensor("stream", [128, NFC * TC], F32)

    def strm(fc, lo, hi):
        return stream[:, fc * TC + lo: fc * TC + hi]

    st5 = ExitStack()
    hi_sb = st5.enter_context(nc.sbuf_tensor("hi_sb5", [128, NFC * TC], BF16))
    lo_sb = st5.enter_context(nc.sbuf_tensor("lo_sb5", [128, NFC * TC], BF16))
    xt_f = st5.enter_context(nc.sbuf_tensor("xt_f5", [128, D], F32))
    xt_b = st5.enter_context(nc.sbuf_tensor("xt_b5", [128, D], BF16))
    lo_t = st5.enter_context(nc.sbuf_tensor("lo_t5", [128, D], BF16))

    c = gp.dma_start(out=mybf_d.ap(), in_=xmy.ap())
    hi_gd = s_gd.inc(c, 16)
    dwait(sy, s_gd)
    for fc in range(NFC):
        c = sy.dma_start(out=hi_sb[:, fc * TC:(fc + 1) * TC],
                         in_=mybf_d[:, fc * 128:(fc + 1) * 128], transpose=True)
        s_ld.inc(c, 16)
    for tt in range(16):
        r0, r1 = tt * 128, (tt + 1) * 128
        c = sy.dma_start(out=xt_f[:], in_=xmy[r0:r1, :]); s_ld.inc(c, 16)
        c = sy.dma_start(out=xt_b[:], in_=mybf_d[r0:r1, :]); s_ld.inc(c, 16)
        dwait(ve, s_ld)
        c = ve.tensor_sub(lo_t[:], xt_f[:], xt_b[:]); dvv = s_dv.inc(c)
        sy.wait_ge(s_dv.h, dvv)
        c = sy.dma_start(out=xbf_d[r0:r1, 0:D], in_=lo_t[:]); s_ld.inc(c, 16)
    dwait(sy, s_ld)
    for fc in range(NFC):
        c = sy.dma_start(out=lo_sb[:, fc * TC:(fc + 1) * TC],
                         in_=xbf_d[0:TC, fc * 128:(fc + 1) * 128], transpose=True)
        s_ld.inc(c, 16)
    dwait(ve, s_ld)
    for fc in range(NFC):
        c = ve.tensor_add(strm(fc, 0, TC), hi_sb[:, fc * TC:(fc + 1) * TC],
                          lo_sb[:, fc * TC:(fc + 1) * TC])
    stream_x = s_dv.inc(c)
    st5.close()
    phase_barrier()

    st5b = ExitStack()
    pool5b = st5b.enter_context(nc.psum_tensor("pool5b", [128, 2048], F32))
    wo_sb = st5b.enter_context(nc.sbuf_tensor("wo_sb5", [128, 16 * D], BF16))
    og_sb = st5b.enter_context(nc.sbuf_tensor("og_sb5", [128, 16 * D], BF16))
    onw_c = st5b.enter_context(nc.sbuf_tensor("onw_c5", [128, 4], F32))
    r1p1b = st5b.enter_context(nc.sbuf_tensor("r1p1b5", [128, TC], F32))
    sq5 = st5b.enter_context(nc.sbuf_tensor("sq5", [128, 512], BF16))

    c = gp.dma_start(out=wo_sb[:].rearrange("p (c n) -> p c n", c=16),
                     in_=wo.ap().rearrange("(c p) n -> p c n", p=128))
    s_gd.inc(c, 16)
    c = sy.dma_start(out=onw_c[:], in_=onw.ap())
    s_ld.inc(c, 16)
    dwait(ve, s_gd)
    dwait(ve, s_ld)
    for kc in range(16):
        c = ve.tensor_scalar_mul(wo_sb[:, kc * D:(kc + 1) * D], wo_sb[:, kc * D:(kc + 1) * D],
                                 onw_c[:, kc % 4:kc % 4 + 1])
    wo_dv = s_dv.inc(c)

    ve.wait_ge(s_dv.h, stream_x)
    stats_rows(lambda fc, a, b_: strm(fc, a, b_), TC,
               pool5b[0:1, 0:512], pool5b[0:1, 512:1024], sq5, True, 8, 9)
    lnrows(TC, fused=False)
    load_rowbf(1, 11, TC)
    bv = bcast_tile(r1p1b[:], 12, 0, TC)
    dwait(ve, s_gd)
    dwait(ve, s_ld)
    te.wait_ge(s_dv.h, wo_dv)
    if "nocc" not in dbg:
        dwait(sy, s_cc)
        pid5 = nc.partition_id(engines=[mybir.EngineType.SP])
        slot5 = pid5 % 4
        for hb in range(4):
            c = sy.dma_start(out=og2_d[hb * TC:(hb + 1) * TC, :],
                             in_=og_d[bass.ts(slot5 + 4 * hb, TC), :])
            s_ld.inc(c, 16)
    for ct in range(2):
        t0 = ct * 1024
        for hb in range(4):
            for dc in range(4):
                c = sy.dma_start(out=og_sb[:, (hb * 4 + dc) * 1024:(hb * 4 + dc + 1) * 1024],
                                 in_=og2_d[hb * TC + t0:hb * TC + t0 + 1024,
                                           dc * 128:(dc + 1) * 128],
                                 transpose=True)
                s_ld.inc(c, 16)
        dwait(te, s_ld)
        for oc in range(NFC):
            for colt in range(2):
                cl, chh = t0 + colt * 512, t0 + (colt + 1) * 512
                pz = pool5b[:, 1024 + (colt % 2) * 512: 1024 + (colt % 2) * 512 + 512]
                for kc in range(16):
                    MM(pz, wo_sb[:, kc * D + oc * 128: kc * D + (oc + 1) * 128],
                       og_sb[:, kc * 1024 + colt * 512: kc * 1024 + (colt + 1) * 512],
                       kc == 0, False)
                MM(pz, ones_r[0:1, 0:128], rb(1, cl, chh), False, True)
                pv = s_pe.inc(MML[0])
                ve.wait_ge(s_pe.h, pv)
                ve.tensor_mul(strm(oc, cl, chh), strm(oc, cl, chh), r1p1b[:, cl:chh])
                c = ve.tensor_add(strm(oc, cl, chh), strm(oc, cl, chh), pz)
                dvv = s_dv.inc(c)
                te.wait_ge(s_dv.h, dvv)
        sy.wait_ge(s_dv.h, s_dv.n)
    c = ve.engine_nop(); x1_dv = s_dv.inc(c)
    if "x1" in taps:
        sy.wait_ge(s_dv.h, x1_dv)
        for fc in range(NFC):
            c = sy.dma_start(out=tap_t["x1"][fc * 128:(fc + 1) * 128, :], in_=strm(fc, 0, TC))
            s_st.inc(c, 16)
    st5b.close()

    def emit_ffn(tag, dequant):
        phase_barrier()
        stF = ExitStack()
        poolF = stF.enter_context(nc.psum_tensor("poolF" + tag, [128, 4096], F32))
        sqF = stF.enter_context(nc.sbuf_tensor("sqF" + tag, [128, 512], BF16))
        gcol = stF.enter_context(nc.sbuf_tensor("gcol" + tag, [1, 4096], BF16))
        f2br = stF.enter_context(nc.sbuf_tensor("f2br" + tag, [1, D], BF16))

        if dequant:
            c = sy.dma_start(out=f1b_col[:], in_=f1b.ap())
            s_ld.inc(c, 16)
            c = sy.dma_start(out=fgam_col[:], in_=fgam.ap())
            s_ld.inc(c, 16)
            c = sy.dma_start(out=fbet_col[:], in_=fbet.ap())
            s_ld.inc(c, 16)
            c = gp.dma_start(out=fgam_cb[:], in_=fgam.ap())
            s_gd.inc(c, 16)
            dwait(sy, s_gd)
            dwait(sy, s_ld)
            stDq = ExitStack()
            qbuf = [stDq.enter_context(nc.sbuf_tensor(f"qb{tag}{i}", [128, 4096], I32))
                    for i in range(2)]
            wtile = stDq.enter_context(nc.sbuf_tensor("wtile" + tag, [128, 4096], BF16))
            srow = stDq.enter_context(nc.sbuf_tensor("srow" + tag, [1, 4096], BF16))
            racc = stDq.enter_context(nc.sbuf_tensor("racc" + tag, [1, 4096], F32))
            pqv = [poolF[:, 0:512], poolF[:, 512:1024]]
            pcs = poolF[0:1, 1024:1536]
            scr_dma = s_ld.n
            for fc in range(NFC):
                par = fc % 2
                c = sy.dma_start(out=qbuf[par][:], in_=f1q[fc * 128:(fc + 1) * 128, :])
                dmv = WSEM[par].inc(c, 16)
                c = gp.dma_start(out=srow[:], in_=f1s[fc:fc + 1, :])
                sdv = s_gd.inc(c, 16)
                dwait(ve, WSEM[par])
                dwait(ve, s_ld)
                dwait(te, s_gd)
                for nt in range(8):
                    nl, nh = nt * 512, (nt + 1) * 512
                    MM(pqv[par], ones_r[0:1, 0:128], srow[0:1, nl:nh], True, True)
                    pv = s_pe.inc(MML[0])
                    ve.wait_ge(s_pe.h, pv)
                    c = ve.scalar_tensor_tensor(wtile[:, nl:nh], qbuf[par][:, nl:nh], 128.0,
                                                pqv[par], ALU.subtract, ALU.mult)
                    dvv = s_dv.inc(c)
                    te.wait_ge(s_dv.h, dvv)
                c = ve.engine_nop(); wt_dv = s_dv.inc(c)
                te.wait_ge(s_dv.h, wt_dv)
                for nt in range(8):
                    nl, nh = nt * 512, (nt + 1) * 512
                    MM(pcs, fgam_cb[:, fc:fc + 1], wtile[:, nl:nh], True, True)
                    pv = s_pe.inc(MML[0])
                    ve.wait_ge(s_pe.h, pv)
                    if fc == 0:
                        c = ve.tensor_copy(racc[0:1, nl:nh], pcs)
                    else:
                        c = ve.tensor_add(racc[0:1, nl:nh], racc[0:1, nl:nh], pcs)
                    dvv = s_dv.inc(c)
                    te.wait_ge(s_dv.h, dvv)
                sy.wait_ge(s_dv.h, wt_dv)
                c = sy.dma_start(out=w1_scr[fc * 128:(fc + 1) * 128, :], in_=wtile[:])
                scr_dma = s_ld.inc(c, 16)
                gp.wait_ge(s_pe.h, s_pe.n)
            c = ve.tensor_copy(gcol[0:1, 0:4096], racc[0:1, 0:4096]); gcol_dv = s_dv.inc(c)
            sy.wait_ge(s_dv.h, gcol_dv)
            c = sy.dma_start(out=row_d[15:16, 0:4096], in_=racc[:])
            s_ld.inc(c, 16)
            for kc in range(32):
                par = kc % 2
                c = sy.dma_start(out=qbuf[par][:, 0:1024], in_=f2q[kc * 128:(kc + 1) * 128, :])
                dmv = WSEM[par].inc(c, 16)
                c = gp.dma_start(out=srow[0:1, 0:1024], in_=f2s[kc:kc + 1, :])
                sdv = s_gd.inc(c, 16)
                dwait(ve, WSEM[par])
                dwait(ve, s_ld)
                dwait(te, s_gd)
                for nt in range(2):
                    nl, nh = nt * 512, (nt + 1) * 512
                    MM(pqv[par], ones_r[0:1, 0:128], srow[0:1, nl:nh], True, True)
                    pv = s_pe.inc(MML[0])
                    ve.wait_ge(s_pe.h, pv)
                    c = ve.scalar_tensor_tensor(wtile[:, nl:nh], qbuf[par][:, nl:nh], 128.0,
                                                pqv[par], ALU.subtract, ALU.mult)
                    dvv = s_dv.inc(c)
                    te.wait_ge(s_dv.h, dvv)
                c = ve.engine_nop(); wt_dv = s_dv.inc(c)
                sy.wait_ge(s_dv.h, wt_dv)
                c = sy.dma_start(out=w2_scr[kc * 128:(kc + 1) * 128, :], in_=wtile[:, 0:1024])
                scr_dma = s_ld.inc(c, 16)
                gp.wait_ge(s_pe.h, s_pe.n)
            dwait(sy, s_ld)
            stDq.close()
        else:
            dwait(gp, s_ld)
            c = gp.dma_start(out=gcol[0:1, 0:4096], in_=row_d[15:16, 0:4096])
            s_gd.inc(c, 16)
        c = gp.dma_start(out=f2br[:], in_=f2b.ap().rearrange("(o n) -> o n", o=1))
        f2b_gd = s_gd.inc(c, 16)

        stats_rows(lambda fc, a, b_: strm(fc, a, b_), TC,
                   poolF[0:1, 0:512], poolF[0:1, 512:1024], sqF, True, 8, 9)
        lnrows(TC, fused=True, kmul=2.0)
        if tag == "B" and "br" in taps:
            dwait(sy, s_ld)
            c = sy.dma_start(out=tap_t["br"].ap(), in_=row_d.ap()[0:16, :])
            s_st.inc(c, 16)
            c = sy.dma_start(out=tap_t["br"].ap()[0:16, 2048:2944], in_=rmath[0:16, :])
            s_st.inc(c, 16)
        load_rowbf(0, 11, TC)
        load_rowbf(1, 13, TC)

        hf = stF.enter_context(nc.sbuf_tensor("hf" + tag, [128, NFC * 512], BF16))
        Rbt = stF.enter_context(nc.sbuf_tensor("Rbt" + tag, [128, 512], F32))
        y_sb = stF.enter_context(nc.sbuf_tensor("y" + tag, [128, 32 * 512], BF16))
        w1t = [stF.enter_context(nc.sbuf_tensor(f"w1t{tag}{i}", [128, 8 * 1024], BF16))
               for i in range(2)]
        w2t = [stF.enter_context(nc.sbuf_tensor(f"w2t{tag}{i}", [128, 4 * 1024], BF16))
               for i in range(2)]

        dwait(te, s_ld)
        dwait(ve, s_ld)
        dwait(te, s_gd)
        z_dv = s_dv.n
        for ts in range(4):
            cl, chh = ts * 512, (ts + 1) * 512
            bv = bcast_tile(Rbt[:], 10, cl, chh)
            dwait(ve, s_gd)
            ve.wait_ge(s_dv.h, z_dv)
            for fc in range(NFC):
                ve.scalar_tensor_tensor(hf[:, fc * 512:(fc + 1) * 512], strm(fc, cl, chh),
                                        fgam_col[:, fc:fc + 1], Rbt[:], ALU.mult, ALU.mult)
                c = ve.tensor_scalar_add(hf[:, fc * 512:(fc + 1) * 512],
                                         hf[:, fc * 512:(fc + 1) * 512],
                                         fbet_col[:, fc:fc + 1])
            hf_dv = s_dv.inc(c)
            if ts == 0 and tag == "B" and "b0" in taps:
                gp.wait_ge(s_dv.h, hf_dv)
                c = gp.dma_start(out=tap_t["b0"].ap()[:, 0:4096], in_=hf[:])
                s_gd.inc(c, 16)
                c = gp.dma_start(out=tap_t["b0"].ap()[0:1, 4096:8192], in_=gcol[0:1, 0:4096])
                s_gd.inc(c, 16)
                c = gp.dma_start(out=tap_t["b0"].ap()[0:1, 8192:10240], in_=rb(0, 0, TC))
                s_gd.inc(c, 16)
                c = gp.dma_start(out=tap_t["b0"].ap()[0:1, 10240:12288], in_=rb(1, 0, TC))
                s_gd.inc(c, 16)
                dwait(gp, s_gd)
            te.wait_ge(s_dv.h, hf_dv)
            for grp in range(4):
                par = grp % 2
                g = par_guard.get(("w1t", tag, par))
                if g:
                    sy.wait_ge(s_pe.h, g)
                c = sy.dma_start(
                    out=w1t[par][:].rearrange("p (c n) -> p c n", c=8),
                    in_=w1_scr.ap().rearrange("(c p) n -> p c n", p=128)[:, :, grp * 1024:(grp + 1) * 1024])
                wv_ = WSEM[par].inc(c, 16)
                dwait(te, WSEM[par])
                for fc in range(NFC):
                    for mo in range(8):
                        MM(poolF[:, mo * 512:(mo + 1) * 512],
                           w1t[par][:, fc * 1024 + mo * 128: fc * 1024 + (mo + 1) * 128],
                           hf[:, fc * 512:(fc + 1) * 512], fc == 0, False)
                for mo in range(8):
                    moc = grp * 8 + mo
                    MM(poolF[:, mo * 512:(mo + 1) * 512],
                       gcol[0:1, moc * 128:(moc + 1) * 128], rb(0, cl, chh), False, True)
                pv = s_pe.inc(MML[0])
                par_guard[("w1t", tag, par)] = pv
                sc.wait_ge(s_pe.h, pv)
                for mo in range(8):
                    moc = grp * 8 + mo
                    c = sc.activation(y_sb[:, moc * 512:(moc + 1) * 512],
                                      poolF[:, mo * 512:(mo + 1) * 512], AF.Silu,
                                      bias=f1b_col[:, moc:moc + 1])
                av = s_ac.inc(c)
                te.wait_ge(s_ac.h, av)
            if ts == 0 and tag == "B" and "b0" in taps:
                ve.wait_ge(s_ac.h, s_ac.n)
                c = ve.engine_nop(); _b0 = s_dv.inc(c)
                gp.wait_ge(s_dv.h, _b0)
                c = gp.dma_start(out=tap_t["b0"].ap()[:, 12288:14336], in_=y_sb[:, 0:2048])
                s_gd.inc(c, 16)
                dwait(gp, s_gd)
            for kc in range(32):
                par = (kc // 4) % 2
                if kc % 4 == 0:
                    g = par_guard.get(("w2t", tag, par))
                    if g:
                        sy.wait_ge(s_pe.h, g)
                    c = sy.dma_start(
                        out=w2t[par][:].rearrange("p (c n) -> p c n", c=4),
                        in_=w2_scr.ap().rearrange("(c p) n -> p c n", p=128)[:, kc:kc + 4, :])
                    wv_ = WSEM[par].inc(c, 16)
                    dwait(te, WSEM[par])
                for zc in range(8):
                    MM(poolF[:, zc * 512:(zc + 1) * 512],
                       w2t[par][:, (kc % 4) * 1024 + zc * 128:(kc % 4) * 1024 + (zc + 1) * 128],
                       y_sb[:, kc * 512:(kc + 1) * 512], kc == 0, False)
                if kc % 4 == 3:
                    par_guard[("w2t", tag, par)] = s_pe.inc(MML[0])
            for zc in range(8):
                MM(poolF[:, zc * 512:(zc + 1) * 512], f2br[0:1, zc * 128:(zc + 1) * 128],
                   ones_r[0:1, 0:512], False, False)
                MM(poolF[:, zc * 512:(zc + 1) * 512], ones_r[0:1, 0:128], rb(1, cl, chh),
                   False, True)
            pv = s_pe.inc(MML[0])
            gp.wait_ge(s_dv.h, s_dv.n)
            bv = bcast_tile(Rbt[:], 12, cl, chh)
            dwait(ve, s_gd)
            ve.wait_ge(s_pe.h, pv)
            for zc in range(8):
                ve.tensor_mul(strm(zc, cl, chh), strm(zc, cl, chh), Rbt[:])
                c = ve.scalar_tensor_tensor(strm(zc, cl, chh), poolF[:, zc * 512:(zc + 1) * 512],
                                            0.5, strm(zc, cl, chh), ALU.mult, ALU.add)
            z_dv = s_dv.inc(c)
            te.wait_ge(s_dv.h, z_dv)
            gp.wait_ge(s_dv.h, z_dv)
        stF.close()
        phase_barrier()
        c = ve.engine_nop()
        return s_dv.inc(c)

    x2_dv = s_dv.n
    if upto >= 2:
        x2_dv = emit_ffn("B", dequant=True)
    if "x2" in taps:
        sy.wait_ge(s_dv.h, x2_dv)
        for fc in range(NFC):
            c = sy.dma_start(out=tap_t["x2"][fc * 128:(fc + 1) * 128, :], in_=strm(fc, 0, TC))
            s_st.inc(c, 16)

    phase_barrier()
    stC = ExitStack()
    SKIP_C = upto < 3
    if not SKIP_C:
        poolC = stC.enter_context(nc.psum_tensor("poolC", [128, 4096], F32))
        lmask = stC.enter_context(nc.sbuf_tensor("lmaskt", [128, 2 * 768], BF16))
        latm = stC.enter_context(nc.sbuf_tensor("latmt", [64, TC], BF16))
        dloc = stC.enter_context(nc.sbuf_tensor("dloct", [1, TC], BF16))
        dlat = stC.enter_context(nc.sbuf_tensor("dlatt", [1, TC], BF16))
        hsel = stC.enter_context(nc.sbuf_tensor("hselt", [128, 4], F32))
        colqkv = stC.enter_context(nc.sbuf_tensor("colqkvC", [1, 3 * D + 256], BF16))
        sqC = stC.enter_context(nc.sbuf_tensor("sqC", [128, 512], BF16))
        for dst, src in ((dloc, dloc_i), (dlat, dlat_i), (hsel, hsel_i), (latm, latm_i)):
            c = sy.dma_start(out=dst[:], in_=src.ap()); s_ld.inc(c, 16)
        c = sy.dma_start(out=lmask[:].rearrange("p (v n) -> p v n", v=2), in_=lmask_i.ap())
        s_ld.inc(c, 16)

        stats_rows(lambda fc, a, b_: strm(fc, a, b_), TC,
                   poolC[0:1, 0:512], poolC[0:1, 512:1024], sqC, True, 8, 9)
        lnrows(TC, fused=False)
        load_rowbf(1, 11, TC)

        x3q = stC.enter_context(nc.sbuf_tensor("x3qC", [128, NFC * 640], BF16))
        Rbq = stC.enter_context(nc.sbuf_tensor("RbqC", [128, 640], F32))
        halo = stC.enter_context(nc.sbuf_tensor("haloC", [128, NFC * 128], BF16))
        pooled = stC.enter_context(nc.sbuf_tensor("pooledC", [128, NFC * 64], BF16))
        stCp = ExitStack()
        pool_f = stCp.enter_context(nc.sbuf_tensor("pool_fC", [128, 16], F32))
        poolbf = stCp.enter_context(nc.sbuf_tensor("poolbfC", [128, NFC * 16], BF16))
        corr16 = stCp.enter_context(nc.sbuf_tensor("corr16C", [1, 16], F32))
        rrow = stCp.enter_context(nc.sbuf_tensor("rrowC", [1, TC], F32))
        corrb = stCp.enter_context(nc.sbuf_tensor("corrbC", [128, 16], F32))
        bnd = stCp.enter_context(nc.sbuf_tensor("bndC", [128, NFC * 128], BF16))
        corrbb = stCp.enter_context(nc.sbuf_tensor("corrbbC", [128, 128], F32))

        def build_x3q(qtr):
            t0 = qtr * 512
            if qtr > 0:
                for fc in range(NFC):
                    c = ve.tensor_copy(x3q[:, fc * 640: fc * 640 + 128],
                                       x3q[:, fc * 640 + 512: fc * 640 + 640])
                s_dv.inc(c)
            gp.wait_ge(s_dv.h, s_dv.n)
            bv = bcast_tile(Rbq[:, 128:640], 10, t0, t0 + 512)
            dwait(ve, s_gd)
            for fc in range(NFC):
                c = ve.tensor_mul(x3q[:, fc * 640 + 128: fc * 640 + 640],
                                  strm(fc, t0, t0 + 512), Rbq[:, 128:640])
            return s_dv.inc(c)

        c = sy.dma_start(out=rrow[:], in_=row_d[11:12, 0:TC]); s_ld.inc(c, 16)
        dwait(ve, s_ld)
        c = ve.tensor_reduce(corr16[0:1, 0:16].rearrange("o (g n) -> o g n", g=16),
                             rrow[:].rearrange("o (g n) -> o g n", g=16), AXX, ALU.add)
        cdv = s_dv.inc(c)
        sy.wait_ge(s_dv.h, cdv)
        c = sy.dma_start(out=row_d[14:15, 0:16], in_=corr16[:]); s_ld.inc(c, 16)
        bv = bcast_tile(corrb[:], 14, 0, 16)
        dwait(ve, s_gd)
        for qtr in range(4):
            xq_dv = build_x3q(qtr)
            ve.wait_ge(s_dv.h, xq_dv)
            for fc in range(NFC):
                for g in range(4):
                    c = ve.reduce_sum(pool_f[:, qtr * 4 + g:qtr * 4 + g + 1],
                                      x3q[:, fc * 640 + 128 + g * 128: fc * 640 + 128 + (g + 1) * 128],
                                      AXX)
                rv_ = s_dv.inc(c)
                ve.wait_ge(s_dv.h, rv_)
                ve.tensor_add(pool_f[:, qtr * 4:qtr * 4 + 4], pool_f[:, qtr * 4:qtr * 4 + 4],
                              corrb[:, qtr * 4:qtr * 4 + 4])
                c = ve.tensor_scalar_mul(poolbf[:, fc * 16 + qtr * 4: fc * 16 + (qtr + 1) * 4],
                                         pool_f[:, qtr * 4:qtr * 4 + 4], 1.0 / 128.0)
            pl_dv = s_dv.inc(c)
            if qtr == 3:
                bv = bcast_tile(corrbb[:], 11, TC - 128, TC)
                dwait(ve, s_gd)
                for fc in range(NFC):
                    c = ve.tensor_add(bnd[:, fc * 128:(fc + 1) * 128],
                                      x3q[:, fc * 640 + 512: fc * 640 + 640], corrbb[:])
                pl_dv = s_dv.inc(c)
        sy.wait_ge(s_dv.h, s_dv.n)
        for fc in range(NFC):
            c = sy.dma_start(out=cc2i_d[fc * 128:(fc + 1) * 128, 0:128],
                             in_=bnd[:, fc * 128:(fc + 1) * 128]); s_ld.inc(c, 16)
            c = sy.dma_start(out=cc2i_d[fc * 128:(fc + 1) * 128, 128:144],
                             in_=poolbf[:, fc * 16:(fc + 1) * 16]); s_ld.inc(c, 16)
        dwait(gp, s_ld)
        c = gp.collective_compute("AllGather", ALU.bypass,
                                  replica_groups=[[0, 1, 2, 3], [4, 5, 6, 7]],
                                  ins=[cc2i_d.ap().opt()], outs=[cc2o_d.ap().opt()])
        cc2 = s_cc.inc(c)

        hblk = bnd
        dwait(sy, s_cc)
        for blk in range(4):
            for fc in range(NFC):
                c = sy.dma_start(out=pooled[:, fc * 64 + blk * 16: fc * 64 + (blk + 1) * 16],
                                 in_=cc2o_d[blk * D + fc * 128:blk * D + (fc + 1) * 128, 128:144])
                s_ld.inc(c, 16)
        dwait(ve, s_ld)
        c = ve.memset(halo[:], 0.0); s_dv.inc(c)
        for blk in range(3):
            for fc in range(NFC):
                c = sy.dma_start(out=hblk[:, fc * 128:(fc + 1) * 128],
                                 in_=cc2o_d[blk * D + fc * 128:blk * D + (fc + 1) * 128, 0:128])
                s_ld.inc(c, 16)
            dwait(ve, s_ld)
            for fc in range(NFC):
                c = ve.scalar_tensor_tensor(halo[:, fc * 128:(fc + 1) * 128],
                                            hblk[:, fc * 128:(fc + 1) * 128],
                                            hsel[:, blk:blk + 1], halo[:, fc * 128:(fc + 1) * 128],
                                            ALU.mult, ALU.add)
            hdv = s_dv.inc(c)
            sy.wait_ge(s_dv.h, hdv)
        halo_dv = s_dv.n
        stCp.close()

        qkvb_col = stC.enter_context(nc.sbuf_tensor("qkvb_colC", [128, 16], F32))
        vbias_r = stC.enter_context(nc.sbuf_tensor("vbias_rC", [1, D], BF16))
        rqb_col = stC.enter_context(nc.sbuf_tensor("rqb_colC", [128, 2], F32))
        rkvb_col = stC.enter_context(nc.sbuf_tensor("rkvb_colC", [128, 4], F32))
        routb_col = stC.enter_context(nc.sbuf_tensor("routb_colC", [128, 8], F32))
        outb_row = stC.enter_context(nc.sbuf_tensor("outb_rowC", [1, D], BF16))
        glw_sb = stC.enter_context(nc.sbuf_tensor("glw_sbC", [64, 1], BF16))
        grw_sb = stC.enter_context(nc.sbuf_tensor("grw_sbC", [64, 1], BF16))
        rkvb_row = stC.enter_context(nc.sbuf_tensor("rkvb_rowC", [1, 512], BF16))
        c = sy.dma_start(out=qkvb_col[:], in_=qkvb_c.ap())
        s_ld.inc(c, 16)
        c = gp.dma_start(out=vbias_r[:], in_=qkvb.ap()[2048:3072].rearrange("(o n) -> o n", o=1))
        s_gd.inc(c, 16)
        c = sy.dma_start(out=rqb_col[:], in_=rqb.ap())
        s_ld.inc(c, 16)
        c = sy.dma_start(out=rkvb_col[:], in_=rkvb_c.ap())
        s_ld.inc(c, 16)
        c = sy.dma_start(out=routb_col[:], in_=routb.ap())
        s_ld.inc(c, 16)
        c = gp.dma_start(out=outb_row[:], in_=outb.ap().rearrange("(o n) -> o n", o=1))
        s_gd.inc(c, 16)
        c = gp.dma_start(out=glw_sb[:], in_=glw.ap()); s_gd.inc(c, 16)
        c = gp.dma_start(out=grw_sb[:], in_=grw.ap()); s_gd.inc(c, 16)
        c = gp.dma_start(out=rkvb_row[:], in_=rkvb.ap().rearrange("(o n) -> o n", o=1))
        s_gd.inc(c, 16)
        dwait(ve, s_ld)
        dwait(ve, s_gd)
        c = ve.tensor_scalar_mul(rqb_col[:], rqb_col[:], 0.125); s_dv.inc(c)

        rk_fm = stC.enter_context(nc.sbuf_tensor("rk_fmC", [128, 128], BF16))
        rv_aug = stC.enter_context(nc.sbuf_tensor("rv_augC", [64, 4 * 65], BF16))
        stCr = ExitStack()
        rkvw_sb = stCr.enter_context(nc.sbuf_tensor("rkvw_sbC", [128, 8 * 512], BF16))
        rkv_f = stCr.enter_context(nc.sbuf_tensor("rkv_fC", [128, 2 * 64], BF16))
        dwait(gp, s_ld)
        gp.wait_ge(s_dv.h, halo_dv)
        c = gp.dma_start(out=rkvw_sb[:].rearrange("p (c n) -> p c n", c=8),
                         in_=rkvw.ap().rearrange("(c p) n -> p c n", p=128))
        rkvw_gd = s_gd.inc(c, 16)
        dwait(te, s_gd)
        te.wait_ge(s_dv.h, s_dv.n)
        for oc in range(2):
            for fc in range(NFC):
                MM(poolC[:, 1024 + oc * 64:1024 + (oc + 1) * 64],
                   rkvw_sb[:, fc * 512 + oc * 128: fc * 512 + (oc + 1) * 128],
                   pooled[:, fc * 64:(fc + 1) * 64], fc == 0, fc == NFC - 1)
        for fc in range(NFC):
            MM(poolC[0:64, 1536:1792], pooled[:, fc * 64:(fc + 1) * 64],
               rkvw_sb[:, fc * 512 + 256:(fc + 1) * 512], fc == 0, False)
        MM(poolC[0:64, 1536:1792], ones_r[0:1, 0:64], rkvb_row[0:1, 256:512], False, True)
        pv = s_pe.inc(MML[0])
        sc.wait_ge(s_pe.h, pv)
        for oc in range(2):
            c = sc.activation(rkv_f[:, oc * 64:(oc + 1) * 64],
                              poolC[:, 1024 + oc * 64:1024 + (oc + 1) * 64],
                              AF.Identity, bias=rkvb_col[:, oc:oc + 1])
        c = sc.activation(rv_aug[0:64, 0:260].rearrange("p (h n) -> p h n", h=4)[:, :, 0:64],
                          poolC[0:64, 1536:1792].rearrange("p (h n) -> p h n", h=4), AF.Identity)
        av = s_ac.inc(c)
        ve.wait_ge(s_ac.h, av)
        ve.tensor_copy(rk_fm[:], rkv_f[:, 0:128])
        ve.memset(rv_aug[0:64, 64:65], 1.0)
        ve.memset(rv_aug[0:64, 129:130], 1.0)
        ve.memset(rv_aug[0:64, 194:195], 1.0)
        c = ve.memset(rv_aug[0:64, 259:260], 1.0)
        rv_dv = s_dv.inc(c)
        te.wait_ge(s_dv.h, rv_dv)
        stCr.close()

        vw_sb = stC.enter_context(nc.sbuf_tensor("vw_sbC", [128, 8 * 1024], BF16))
        routw_sb = stC.enter_context(nc.sbuf_tensor("routw_sbC", [128, 2 * D], BF16))
        qkvw_t = [stC.enter_context(nc.sbuf_tensor(f"qkvwtC{i}", [128, 2048 + 256], BF16))
                  for i in range(2)]
        gp.wait_ge(s_dv.h, rv_dv)
        c = gp.dma_start(out=vw_sb[:].rearrange("p (c n) -> p c n", c=NFC),
                         in_=qkvw.ap()[:, 2048:3072].rearrange("(c p) n -> p c n", p=128))
        s_gd.inc(c, 16)
        c = gp.dma_start(out=routw_sb[:].rearrange("p (c n) -> p c n", c=2),
                         in_=routw.ap().rearrange("(c p) n -> p c n", p=128))
        vw_gd = s_gd.inc(c, 16)

        for fc in range(NFC):
            par = fc % 2
            g = par_guard.get(("qkvw", par))
            if g:
                gp.wait_ge(s_pe.h, g)
            c = gp.dma_start(out=qkvw_t[par][:, 0:2048], in_=qkvw[fc * 128:(fc + 1) * 128, 0:2048])
            s_gd.inc(c, 16)
            c = gp.dma_start(out=qkvw_t[par][:, 2048:2304], in_=rqw[fc * 128:(fc + 1) * 128, :])
            wv_ = s_gd.inc(c, 16)
            dwait(te, s_gd)
            for nt in range(5):
                nl = nt * 512
                nh = min(nl + 512, 2304)
                MM(poolC[0:1, 1024 + nt * 512:1024 + nt * 512 + (nh - nl)], ones_c[:, 0:1],
                   qkvw_t[par][:, nl:nh], fc == 0, fc == NFC - 1)
            pv = s_pe.inc(MML[0])
            par_guard[("qkvw", par)] = pv
            if fc < NFC - 1:
                gp.wait_ge(s_pe.h, pv)
        sc.wait_ge(s_pe.h, s_pe.n)
        for nt in range(5):
            nl = nt * 512
            nh = min(nl + 512, 2304)
            dst = colqkv[0:1, nl:nh] if nh <= 2048 else colqkv[0:1, 3072:3072 + 256]
            c = sc.copy(dst, poolC[0:1, 1024 + nt * 512:1024 + nt * 512 + (nh - nl)])
        colq_ac = s_ac.inc(c)
        te.wait_ge(s_ac.h, colq_ac)
        dwait(te, s_gd)
        for nt in range(2):
            for fc in range(NFC):
                MM(poolC[0:1, 1024 + nt * 512:1536 + nt * 512], ones_c[:, 0:1],
                   vw_sb[:, fc * 1024 + nt * 512: fc * 1024 + (nt + 1) * 512],
                   fc == 0, fc == NFC - 1)
            pv = s_pe.inc(MML[0])
            sc.wait_ge(s_pe.h, pv)
            c = sc.copy(colqkv[0:1, 2048 + nt * 512:2048 + (nt + 1) * 512],
                        poolC[0:1, 1024 + nt * 512:1536 + nt * 512])
            av = s_ac.inc(c)
            te.wait_ge(s_ac.h, av)

        q_fm2 = stC.enter_context(nc.sbuf_tensor("q_fm2C", [128, 8 * 512], BF16))
        k_fm2 = stC.enter_context(nc.sbuf_tensor("k_fm2C", [128, 8 * 640], BF16))
        v_aug = stC.enter_context(nc.sbuf_tensor("v_augC", [128, 5 * 1040], BF16))
        rq_fm = stC.enter_context(nc.sbuf_tensor("rq_fmC", [128, 2 * 512], BF16))
        mixed = stC.enter_context(nc.sbuf_tensor("mixedC", [128, 8 * 256], BF16))
        remote = stC.enter_context(nc.sbuf_tensor("remoteC", [64, 16 * 256], BF16))
        lat_o = stC.enter_context(nc.sbuf_tensor("lat_oC", [128, 2 * 256], BF16))
        e_sb = stC.enter_context(nc.sbuf_tensor("e_sbC", [128, 768], BF16))
        rb_sb = stC.enter_context(nc.sbuf_tensor("rb_sbC", [64, 256], BF16))
        loc_bf = stC.enter_context(nc.sbuf_tensor("loc_bfC", [64, 256], BF16))
        d_bf = stC.enter_context(nc.sbuf_tensor("d_bfC", [64, 256], BF16))
        d2_bf = stC.enter_context(nc.sbuf_tensor("d2_bfC", [64, 256], BF16))
        g_row = stC.enter_context(nc.sbuf_tensor("g_rowC", [1, 256], BF16))
        r_rowf = stC.enter_context(nc.sbuf_tensor("r_rowfC", [1, 256], F32))
        r_rowb = stC.enter_context(nc.sbuf_tensor("r_rowbC", [1, 256], BF16))
        crow2 = stC.enter_context(nc.sbuf_tensor("crow2C", [1, 640], BF16))
        outw_t = [stC.enter_context(nc.sbuf_tensor(f"outwtC{i}", [128, 1024], BF16)) for i in range(2)]

        pLSC = poolC[0:64, 0:256]
        pSC = poolC[:, 0:768]
        pLO = poolC[:, 1024:1280]
        pG = poolC[0:1, 1536:1792]
        pGB = poolC[:, 2048:2304]
        pRB = poolC[:, 2560:2816]
        pREM = poolC[:, 3072:3328]

        for qtr in range(4):
            t0 = qtr * 512
            xq_dv = build_x3q(qtr)
            if qtr == 0:
                ve.wait_ge(s_dv.h, halo_dv)
                for fc in range(NFC):
                    c = ve.tensor_copy(x3q[:, fc * 640: fc * 640 + 128],
                                       halo[:, fc * 128:(fc + 1) * 128])
                xq_dv = s_dv.inc(c)
            dwait(ve, s_ld)
            if qtr == 0:
                ve.memset(crow2[0:1, 0:128], 0.0)
                c = ve.tensor_copy(crow2[0:1, 128:640], rb(1, 0, 512))
            else:
                c = ve.tensor_copy(crow2[0:1, 0:640], rb(1, t0 - 128, t0 + 512))
            crow2_dv = s_dv.inc(c)

            te.wait_ge(s_dv.h, crow2_dv)
            te.wait_ge(s_dv.h, xq_dv)
            rowchunks = [("q", oc) for oc in range(8)] + [("k", oc) for oc in range(8)] + \
                        [("rq", oc) for oc in range(2)]
            for gi in range(0, 18, 3):
                grp = rowchunks[gi:gi + 3]
                for fc in range(NFC):
                    par = fc % 2
                    g = par_guard.get(("qkvw", par))
                    if g:
                        gp.wait_ge(s_pe.h, g)
                    c = gp.dma_start(out=qkvw_t[par][:, 0:2048],
                                     in_=qkvw[fc * 128:(fc + 1) * 128, 0:2048])
                    s_gd.inc(c, 16)
                    c = gp.dma_start(out=qkvw_t[par][:, 2048:2304],
                                     in_=rqw[fc * 128:(fc + 1) * 128, :])
                    wv_ = s_gd.inc(c, 16)
                    dwait(te, s_gd)
                    for bi, (kind, oc) in enumerate(grp):
                        wcol = oc * 128 if kind == "q" else (1024 + oc * 128 if kind == "k"
                                                             else 2048 + oc * 128)
                        MM(poolC[:, bi * 512:(bi + 1) * 512],
                           qkvw_t[par][:, wcol:wcol + 128],
                           x3q[:, fc * 640 + 128: fc * 640 + 640], fc == 0, False)
                        if kind == "k":
                            MM(poolC[:, 3584 + bi * 128:3584 + (bi + 1) * 128],
                               qkvw_t[par][:, wcol:wcol + 128],
                               x3q[:, fc * 640: fc * 640 + 128], fc == 0, False)
                    par_guard[("qkvw", par)] = s_pe.inc(MML[0])
                for bi, (kind, oc) in enumerate(grp):
                    ccol = oc * 128 if kind == "q" else (1024 + oc * 128 if kind == "k"
                                                         else 3072 + oc * 128)
                    MM(poolC[:, bi * 512:(bi + 1) * 512],
                       colqkv[0:1, ccol:ccol + 128], crow2[0:1, 128:640], False, True)
                    if kind == "k":
                        MM(poolC[:, 3584 + bi * 128:3584 + (bi + 1) * 128],
                           colqkv[0:1, ccol:ccol + 128], crow2[0:1, 0:128], False, True)
                pv = s_pe.inc(MML[0])
                sc.wait_ge(s_pe.h, pv)
                for bi, (kind, oc) in enumerate(grp):
                    src = poolC[:, bi * 512:(bi + 1) * 512]
                    if kind == "q":
                        c = sc.activation(q_fm2[:, oc * 512:(oc + 1) * 512], src,
                                          AF.Identity, bias=qkvb_col[:, oc:oc + 1])
                    elif kind == "k":
                        c = sc.activation(k_fm2[:, oc * 640 + 128:(oc + 1) * 640], src,
                                          AF.Identity, bias=qkvb_col[:, 8 + oc:9 + oc])
                        c = sc.activation(k_fm2[:, oc * 640: oc * 640 + 128],
                                          poolC[:, 3584 + bi * 128:3584 + (bi + 1) * 128],
                                          AF.Identity, bias=qkvb_col[:, 8 + oc:9 + oc])
                    else:
                        c = sc.activation(rq_fm[:, oc * 512:(oc + 1) * 512], src,
                                          AF.Identity, bias=rqb_col[:, oc:oc + 1], scale=0.125)
                av = s_ac.inc(c)
                te.wait_ge(s_ac.h, av)

            for tt in range(5):
                for nh in range(2):
                    for fc in range(NFC):
                        MM(poolC[:, nh * 512:(nh + 1) * 512],
                           x3q[:, fc * 640 + tt * 128: fc * 640 + (tt + 1) * 128],
                           vw_sb[:, fc * 1024 + nh * 512: fc * 1024 + (nh + 1) * 512],
                           fc == 0, False)
                    MM(poolC[:, nh * 512:(nh + 1) * 512],
                       crow2[0:1, tt * 128:(tt + 1) * 128],
                       colqkv[0:1, 2048 + nh * 512:2048 + (nh + 1) * 512], False, False)
                    MM(poolC[:, nh * 512:(nh + 1) * 512], ones_r[0:1, 0:128],
                       vbias_r[0:1, nh * 512:(nh + 1) * 512], False, True)
                pv = s_pe.inc(MML[0])
                sc.wait_ge(s_pe.h, pv)
                for nh in range(2):
                    dst = v_aug[:, tt * 1040 + nh * 520: tt * 1040 + (nh + 1) * 520]
                    c = sc.activation(dst.rearrange("p (h n) -> p h n", h=8)[:, :, 0:64],
                                      poolC[:, nh * 512:(nh + 1) * 512].rearrange(
                                          "p (h n) -> p h n", h=8), AF.Identity)
                av = s_ac.inc(c)
                ve.wait_ge(s_ac.h, av)
                c = ve.memset(v_aug[:, tt * 1040:(tt + 1) * 1040].rearrange(
                    "p (h n) -> p h n", h=16)[:, :, 64:65], 1.0)
                vdv = s_dv.inc(c)
                te.wait_ge(s_dv.h, vdv)

            dv_last = s_dv.n
            for nn in range(2):
                n0 = t0 + nn * 256
                for lh in range(4):
                    boff = (lh % 2) * 64
                    col = (lh // 2) * 64
                    te.wait_ge(s_dv.h, dv_last)
                    MM(pLSC, rk_fm[boff:boff + 64, col:col + 64],
                       rq_fm[boff:boff + 64, (lh // 2) * 512 + nn * 256:(lh // 2) * 512 + (nn + 1) * 256],
                       True, True)
                    pv = s_pe.inc(MML[0])
                    sc.wait_ge(s_pe.h, pv)
                    c = sc.activation(e_sb[0:64, 0:256], pLSC, AF.Exp)
                    av = s_ac.inc(c)
                    ve.wait_ge(s_ac.h, av)
                    c = ve.tensor_mul(e_sb[0:64, 0:256], e_sb[0:64, 0:256], latm[0:64, n0:n0 + 256])
                    edv = s_dv.inc(c)
                    te.wait_ge(s_dv.h, edv)
                    MM(pLO[0:65, 0:256], rv_aug[0:64, lh * 65:(lh + 1) * 65],
                       e_sb[0:64, 0:256], True, True)
                    pv = s_pe.inc(MML[0])
                    ve.wait_ge(s_pe.h, pv)
                    ve.tensor_add(r_rowf[:], pLO[64:65, 0:256], dlat[0:1, n0:n0 + 256])
                    c = ve.reciprocal(r_rowf[:], r_rowf[:])
                    rv_ = s_dv.inc(c)
                    ve.wait_ge(s_dv.h, rv_)
                    c = ve.tensor_copy(r_rowb[:], r_rowf[:]); rdv = s_dv.inc(c)
                    te.wait_ge(s_dv.h, rdv)
                    MM(pRB[0:64, 0:256], ones_r[0:1, 0:64], r_rowb[:], True, True)
                    pv = s_pe.inc(MML[0])
                    sc.wait_ge(s_pe.h, pv)
                    c = sc.copy(rb_sb[:], pRB[0:64, 0:256]); av = s_ac.inc(c)
                    ve.wait_ge(s_ac.h, av)
                    c = ve.tensor_mul(lat_o[boff:boff + 64, (lh // 2) * 256:(lh // 2 + 1) * 256],
                                      pLO[0:64, 0:256], rb_sb[:])
                    dv_last = s_dv.inc(c)
                te.wait_ge(s_dv.h, dv_last)
                for oc in range(NFC):
                    MM(pREM[:, 0:256], routw_sb[:, oc * 128:(oc + 1) * 128], lat_o[:, 0:256],
                       True, False)
                    MM(pREM[:, 0:256], routw_sb[:, D + oc * 128: D + (oc + 1) * 128],
                       lat_o[:, 256:512], False, True)
                    pv = s_pe.inc(MML[0])
                    sc.wait_ge(s_pe.h, pv)
                    sc.activation(remote[0:64, (2 * oc) * 256:(2 * oc + 1) * 256], pREM[0:64, 0:256],
                                  AF.Identity, bias=routb_col[0:64, oc:oc + 1])
                    c = sc.activation(remote[0:64, (2 * oc + 1) * 256:(2 * oc + 2) * 256],
                                      pREM[64:128, 0:256],
                                      AF.Identity, bias=routb_col[64:128, oc:oc + 1])
                    av = s_ac.inc(c)
                    te.wait_ge(s_ac.h, av)
                for h in range(16):
                    oc = h // 2
                    boff = (h % 2) * 64
                    var = 0 if (qtr == 0 and nn == 0) else 1
                    te.wait_ge(s_dv.h, dv_last)
                    for kc in range(3):
                        MM(pSC[:, kc * 256:(kc + 1) * 256],
                           k_fm2[boff:boff + 64, oc * 640 + nn * 256 + kc * 128:
                                 oc * 640 + nn * 256 + (kc + 1) * 128],
                           q_fm2[boff:boff + 64, oc * 512 + nn * 256: oc * 512 + (nn + 1) * 256],
                           True, True)
                    pv = s_pe.inc(MML[0])
                    sc.wait_ge(s_pe.h, pv)
                    c = sc.activation(e_sb[:], pSC, AF.Exp); av = s_ac.inc(c)
                    ve.wait_ge(s_ac.h, av)
                    c = ve.tensor_mul(e_sb[:], e_sb[:], lmask[:, var * 768:(var + 1) * 768])
                    edv = s_dv.inc(c)
                    te.wait_ge(s_dv.h, edv)
                    for kc in range(3):
                        MM(pLO[0:65, 0:256],
                           v_aug[:, (nn * 2 + kc) * 1040 + h * 65: (nn * 2 + kc) * 1040 + (h + 1) * 65],
                           e_sb[:, kc * 256:(kc + 1) * 256], kc == 0, kc == 2)
                    pv = s_pe.inc(MML[0])
                    ve.wait_ge(s_pe.h, pv)
                    ve.tensor_add(r_rowf[:], pLO[64:65, 0:256], dloc[0:1, n0:n0 + 256])
                    c = ve.reciprocal(r_rowf[:], r_rowf[:])
                    rv_ = s_dv.inc(c)
                    ve.wait_ge(s_dv.h, rv_)
                    c = ve.tensor_copy(r_rowb[:], r_rowf[:]); rdv = s_dv.inc(c)
                    te.wait_ge(s_dv.h, rdv)
                    MM(pRB[0:64, 0:256], ones_r[0:1, 0:64], r_rowb[:], True, True)
                    pv = s_pe.inc(MML[0])
                    sc.wait_ge(s_pe.h, pv)
                    c = sc.copy(rb_sb[:], pRB[0:64, 0:256]); av = s_ac.inc(c)
                    ve.wait_ge(s_ac.h, av)
                    c = ve.tensor_mul(loc_bf[:], pLO[0:64, 0:256], rb_sb[:])
                    ldv = s_dv.inc(c)
                    te.wait_ge(s_dv.h, ldv)
                    MM(pG, glw_sb[:, 0:1], loc_bf[:], True, False)
                    MM(pG, grw_sb[:, 0:1], remote[0:64, h * 256:(h + 1) * 256], False, True)
                    pv = s_pe.inc(MML[0])
                    sc.wait_ge(s_pe.h, pv)
                    c = sc.activation(g_row[:], pG, AF.Sigmoid); av = s_ac.inc(c)
                    te.wait_ge(s_ac.h, av)
                    MM(pGB[0:64, 0:256], ones_r[0:1, 0:64], g_row[:], True, True)
                    pv = s_pe.inc(MML[0])
                    ve.wait_ge(s_pe.h, pv)
                    ve.tensor_sub(d_bf[:], loc_bf[:], remote[0:64, h * 256:(h + 1) * 256])
                    ve.tensor_mul(d2_bf[:], d_bf[:], pGB[0:64, 0:256])
                    c = ve.tensor_add(mixed[boff:boff + 64, oc * 256:(oc + 1) * 256],
                                      d2_bf[:], remote[0:64, h * 256:(h + 1) * 256])
                    dv_last = s_dv.inc(c)
                te.wait_ge(s_dv.h, dv_last)
                for pz in range(2):
                    for kc in range(NFC):
                        par = kc % 2
                        g = par_guard.get(("outw", par))
                        if g:
                            gp.wait_ge(s_pe.h, g)
                        c = gp.dma_start(out=outw_t[par][:], in_=outw[kc * 128:(kc + 1) * 128, :])
                        wv_ = s_gd.inc(c, 16)
                        dwait(te, s_gd)
                        for oi in range(4):
                            oc = pz * 4 + oi
                            MM(poolC[:, oi * 512:oi * 512 + 256],
                               outw_t[par][:, oc * 128:(oc + 1) * 128],
                               mixed[:, kc * 256:(kc + 1) * 256], kc == 0, False)
                        par_guard[("outw", par)] = s_pe.inc(MML[0])
                    for oi in range(4):
                        oc = pz * 4 + oi
                        MM(poolC[:, oi * 512:oi * 512 + 256], outb_row[0:1, oc * 128:(oc + 1) * 128],
                           ones_r[0:1, 0:256], False, True)
                    pv = s_pe.inc(MML[0])
                    ve.wait_ge(s_pe.h, pv)
                    for oi in range(4):
                        oc = pz * 4 + oi
                        c = ve.tensor_add(strm(oc, n0, n0 + 256), strm(oc, n0, n0 + 256),
                                          poolC[:, oi * 512:oi * 512 + 256])
                    zdv = s_dv.inc(c)
                    te.wait_ge(s_dv.h, zdv)
                dv_last = s_dv.n
    stC.close()
    phase_barrier()
    c = ve.engine_nop(); x3o_dv = s_dv.inc(c)
    if "x3" in taps:
        sy.wait_ge(s_dv.h, x3o_dv)
        for fc in range(NFC):
            c = sy.dma_start(out=tap_t["x3"][fc * 128:(fc + 1) * 128, :], in_=strm(fc, 0, TC))
            s_st.inc(c, 16)

    if upto >= 4:
        emit_ffn("Dp", dequant=False)

    phase_barrier()
    stO = ExitStack()
    hi_o = stO.enter_context(nc.sbuf_tensor("hi_oO", [128, NFC * TC], BF16))
    lo_o = stO.enter_context(nc.sbuf_tensor("lo_oO", [128, NFC * TC], BF16))
    ot_f = stO.enter_context(nc.sbuf_tensor("ot_fO", [128, D], F32))
    ot_h = stO.enter_context(nc.sbuf_tensor("ot_hO", [128, D], BF16))
    ot_l = stO.enter_context(nc.sbuf_tensor("ot_lO", [128, D], BF16))
    ve.wait_ge(s_dv.h, s_dv.n)
    for fc in range(NFC):
        ve.tensor_copy(hi_o[:, fc * TC:(fc + 1) * TC], strm(fc, 0, TC))
        c = ve.tensor_sub(lo_o[:, fc * TC:(fc + 1) * TC], strm(fc, 0, TC),
                          hi_o[:, fc * TC:(fc + 1) * TC])
    hl_dv = s_dv.inc(c)
    sy.wait_ge(s_dv.h, hl_dv)
    for fc in range(NFC):
        c = sy.dma_start(out=hi_fd[fc * 128:(fc + 1) * 128, :], in_=hi_o[:, fc * TC:(fc + 1) * TC])
        s_ld.inc(c, 16)
        c = sy.dma_start(out=lo_fd[fc * 128:(fc + 1) * 128, :], in_=lo_o[:, fc * TC:(fc + 1) * TC])
        s_ld.inc(c, 16)
    dwait(sy, s_ld)
    for tt in range(16):
        r0 = tt * 128
        c = sy.dma_start(out=ot_h[:], in_=hi_fd.ap()[:, r0:r0 + 128], transpose=True)
        s_ld.inc(c, 16)
        c = sy.dma_start(out=ot_l[:], in_=lo_fd.ap()[:, r0:r0 + 128], transpose=True)
        s_ld.inc(c, 16)
        dwait(ve, s_ld)
        c = ve.tensor_add(ot_f[:], ot_h[:], ot_l[:]); dvv = s_dv.inc(c)
        sy.wait_ge(s_dv.h, dvv)
        c = sy.dma_start(out=xout[r0:r0 + 128, :], in_=ot_f[:]); s_st.inc(c, 16)
    stO.close()
    dwait(sy, s_st)

    with nc.Block() as block:
        for name, runner in (("sync", block.sync), ("scalar", block.scalar),
                             ("vector", block.vector), ("tensor", block.tensor),
                             ("gpsimd", block.gpsimd)):
            def run(eng, _name=name):
                for meth, a, k, ch in prog[_name]:
                    instr = getattr(eng, meth)(*a, **k)
                    for sem, val in ch.calls:
                        instr.then_inc(sem, val)
            runner(run)
    return nc


_BUILT = {}


def _get_nc(taps=()):
    key = tuple(sorted(taps))
    if key not in _BUILT:
        _BUILT[key] = build(taps=key)
    return _BUILT[key]


def make_in_maps(inputs):
    x = np.ascontiguousarray(inputs["L0_attn_a_x"], dtype=np.float32)
    in_maps = []
    for c in range(NCORES):
        b, s = c // 4, c % 4
        aux = _aux_for_core(c)
        m = dict(
            xb=np.ascontiguousarray(x[b]),
            xmy=np.ascontiguousarray(x[b, s * TC:(s + 1) * TC]),
            wq=np.ascontiguousarray(inputs["L0_attn_a_Wq"][:, s * DKH:(s + 1) * DKH]),
            wk=np.ascontiguousarray(inputs["L0_attn_a_Wk"][:, s * DKH:(s + 1) * DKH]),
            wv=np.ascontiguousarray(inputs["L0_attn_a_Wv"][:, s * DVH:(s + 1) * DVH]),
            wb=np.ascontiguousarray(np.asarray(inputs["L0_attn_a_Wb"])[:, s].reshape(8, 128).T),
            a_lnw=np.ascontiguousarray(inputs["L0_attn_a_ln_w"].reshape(8, 128).T),
            a_lnb=np.ascontiguousarray(inputs["L0_attn_a_ln_b"].reshape(8, 128).T),
            a_onw=np.ascontiguousarray(inputs["L0_attn_a_o_norm_w"].reshape(4, 128).T),
            wo=np.ascontiguousarray(inputs["L0_attn_a_Wo"]),
            qkvw=np.ascontiguousarray(inputs["L0_attn_b_qkv_w"]),
            qkvb=np.ascontiguousarray(inputs["L0_attn_b_qkv_b"]),
            qkvb_c=np.ascontiguousarray(np.asarray(inputs["L0_attn_b_qkv_b"])[0:2048].reshape(16, 128).T),
            rqw=np.ascontiguousarray(inputs["L0_attn_b_rq_w"]),
            rqb=np.ascontiguousarray(np.asarray(inputs["L0_attn_b_rq_b"]).reshape(2, 128).T),
            rkvw=np.ascontiguousarray(inputs["L0_attn_b_rkv_w"]),
            rkvb=np.ascontiguousarray(inputs["L0_attn_b_rkv_b"]),
            rkvb_c=np.ascontiguousarray(np.asarray(inputs["L0_attn_b_rkv_b"]).reshape(4, 128).T),
            routw=np.ascontiguousarray(inputs["L0_attn_b_rout_w"]),
            routb=np.ascontiguousarray(np.asarray(inputs["L0_attn_b_rout_b"]).reshape(8, 128).T),
            outw=np.ascontiguousarray(inputs["L0_attn_b_out_w"]),
            outb=np.ascontiguousarray(inputs["L0_attn_b_out_b"]),
            glw=np.ascontiguousarray(inputs["L0_attn_b_gl_w"]),
            grw=np.ascontiguousarray(inputs["L0_attn_b_gr_w"]),
            fgam=np.ascontiguousarray(np.asarray(inputs["L0_ffn_ln_gamma"]).reshape(8, 128).T),
            fbet=np.ascontiguousarray(np.asarray(inputs["L0_ffn_ln_beta"]).reshape(8, 128).T),
            f1q=np.ascontiguousarray(inputs["L0_ffn_fc1_q"], dtype=np.int32),
            f1s=np.ascontiguousarray(inputs["L0_ffn_fc1_scales"]),
            f1b=np.ascontiguousarray(np.asarray(inputs["L0_ffn_fc1_bias"]).reshape(32, 128).T),
            f2q=np.ascontiguousarray(inputs["L0_ffn_fc2_q"], dtype=np.int32),
            f2s=np.ascontiguousarray(inputs["L0_ffn_fc2_scales"]),
            f2b=np.ascontiguousarray(inputs["L0_ffn_fc2_bias"]),
        )
        m.update(aux)
        in_maps.append({k: np.ascontiguousarray(v) for k, v in m.items()})
    return in_maps


def kernel(**inputs):
    nc = _get_nc(())
    in_maps = make_in_maps(inputs)
    res = run_bass_kernel_spmd(nc, in_maps, core_ids=list(range(NCORES)))
    out = np.empty((B, T, D), np.float32)
    for c in range(NCORES):
        b, s = c // 4, c % 4
        out[b, s * TC:(s + 1) * TC] = res.results[c]["xout"]
    return out
